# version 02
# baseline (speedup 1.0000x reference)
import numpy as np
import ml_dtypes
import concourse.bass as bass
import concourse.mybir as mybir
from concourse.bass_utils import run_bass_kernel_spmd
from contextlib import ExitStack

F32 = mybir.dt.float32
BF16 = mybir.dt.bfloat16
AF = mybir.ActivationFunctionType
ALU = mybir.AluOpType
AX = mybir.AxisListType

T = 4096
D = 4096
OWN = 1024
RS = 0.08838834764831845
EXPB = -14.0
NEXP = 28
RANK = 8
ER = 224

DEBUG = False
STOP_PHASE = 99
_last_results = None

ENGINES = ("tensor", "vector", "scalar", "gpsimd", "sync")


class Sem:
    def __init__(self, h):
        self.h = h
        self.v = 0


class Prog:
    def __init__(self):
        self.streams = {e: [] for e in ENGINES}

    def wait(self, eng, sem, val):
        if val and val > 0:
            self.streams[eng].append(("wait", sem.h, val))

    def op(self, eng, method, *args, sem=None, n=1, **kwargs):
        t = None
        if sem is not None:
            sem.v += n
            t = (sem.h, n)
        self.streams[eng].append(("call", method, args, kwargs, t))
        return sem.v if sem is not None else None


def _replay(E, stream):
    for item in stream:
        if item[0] == "wait":
            E.wait_ge(item[1], item[2])
        else:
            _, method, args, kwargs, t = item
            inst = getattr(E, method)(*args, **kwargs)
            if t is not None:
                inst.then_inc(t[0], t[1])


def own_rows(j):
    gs = np.arange(j, 128, 4)
    return (gs[:, None] * 32 + np.arange(32)[None, :]).reshape(-1)


def _rope_tables():
    inv = 10000.0 ** (-np.arange(64, dtype=np.float64) / 64.0)
    ang = np.arange(T, dtype=np.float64)[:, None] * inv[None, :]
    return (np.cos(ang).T.astype(np.float32),
            np.sin(ang).T.astype(np.float32))


def _masks_for(j):
    p = np.arange(128)
    m = np.zeros((128, 8, 256), dtype=np.float32)
    for r in range(8):
        for u in range(256):
            tp, pq = u // 128, u % 128
            qrel = 512 * tp + 32 * (4 * (pq // 32) + j) + (pq % 32)
            m[:, r, u] = (128 * r + p) <= qrel
    return np.ascontiguousarray(m.reshape(128, 2048)).astype(ml_dtypes.bfloat16)


def build_program():
    nc = bass.Bass("TRN2", target_bir_lowering=False, debug=False)
    scratch_kind = "ExternalOutput" if DEBUG else "Internal"

    d_xt = nc.dram_tensor("xt", [D, T], F32, kind="ExternalInput").ap()
    d_xtq = nc.dram_tensor("xtq", [D, OWN], F32, kind="ExternalInput").ap()
    d_wq = nc.dram_tensor("wq", [D, D], F32, kind="ExternalInput").ap()
    d_wk = nc.dram_tensor("wk", [D, 1024], F32, kind="ExternalInput").ap()
    d_wv = nc.dram_tensor("wv", [D, 1024], F32, kind="ExternalInput").ap()
    d_wo = nc.dram_tensor("wo", [D, D], F32, kind="ExternalInput").ap()
    d_bwT = nc.dram_tensor("bwT", [D, D], F32, kind="ExternalInput").ap()
    d_gwT = nc.dram_tensor("gwT", [D, NEXP], F32, kind="ExternalInput").ap()
    d_aT = nc.dram_tensor("aT", [D, ER], F32, kind="ExternalInput").ap()
    d_bl = nc.dram_tensor("bl", [ER, D], F32, kind="ExternalInput").ap()
    d_cosk = nc.dram_tensor("cosk", [64, T], BF16, kind="ExternalInput").ap()
    d_sink = nc.dram_tensor("sink", [64, T], BF16, kind="ExternalInput").ap()
    d_cosq = nc.dram_tensor("cosq", [64, OWN], BF16, kind="ExternalInput").ap()
    d_sinq = nc.dram_tensor("sinq", [64, OWN], BF16, kind="ExternalInput").ap()
    d_masks = nc.dram_tensor("masks", [128, 2048], BF16, kind="ExternalInput").ap()
    d_ident = nc.dram_tensor("ident", [128, 128], BF16, kind="ExternalInput").ap()
    d_ones = nc.dram_tensor("ones", [128, 128], BF16, kind="ExternalInput").ap()
    d_E = nc.dram_tensor("Emat", [NEXP, ER], BF16, kind="ExternalInput").ap()
    d_out = nc.dram_tensor("out", [OWN, D], F32, kind="ExternalOutput").ap()

    s_xtbf = nc.dram_tensor("s_xtbf", [D, T], BF16, kind=scratch_kind).ap()
    s_kt = nc.dram_tensor("s_kt", [1024, T], BF16, kind=scratch_kind).ap()
    s_v = nc.dram_tensor("s_v", [T, 1024], BF16, kind=scratch_kind).ap()
    s_qt = nc.dram_tensor("s_qt", [D, OWN], BF16, kind=scratch_kind).ap()
    s_hpre = nc.dram_tensor("s_hpre", [D, OWN], BF16, kind=scratch_kind).ap()
    s_h2pre = nc.dram_tensor("s_h2pre", [D, OWN], BF16, kind=scratch_kind).ap()
    if DEBUG:
        s_h1 = nc.dram_tensor("s_h1", [D, OWN], BF16, kind="ExternalOutput").ap()
        s_dbg = nc.dram_tensor("s_dbg", [128, 4096], BF16,
                               kind="ExternalOutput").ap()
        s_dbg2 = nc.dram_tensor("s_dbg2", [128, 256], F32,
                                kind="ExternalOutput").ap()
        s_dbg3 = nc.dram_tensor("s_dbg3", [128, 2048], BF16,
                                kind="ExternalOutput").ap()
        s_h1b = nc.dram_tensor("s_h1b", [D, OWN], BF16,
                               kind="ExternalOutput").ap()

    P = Prog()

    with ExitStack() as st:
        e = st.enter_context

        AR1 = e(nc.sbuf_tensor("AR1", [128, 32768], BF16))
        AR2 = e(nc.sbuf_tensor("AR2", [128, 32768], BF16))
        CSK = e(nc.sbuf_tensor("CSK", [64, 8192], BF16))
        CSQ = e(nc.sbuf_tensor("CSQ", [64, 2048], BF16))
        MSK = e(nc.sbuf_tensor("MSK", [128, 2048], BF16))
        HPB = e(nc.sbuf_tensor("HPB", [128, 4096], BF16))
        KVB = e(nc.sbuf_tensor("KVB", [128, 4096], BF16))
        SQB = e(nc.sbuf_tensor("SQB", [128, 1024], BF16))
        LTM = e(nc.sbuf_tensor("LTM", [128, 1024], BF16))
        IDN = e(nc.sbuf_tensor("IDN", [128, 128], BF16))
        ONE = e(nc.sbuf_tensor("ONE", [128, 128], BF16))
        EMB = e(nc.sbuf_tensor("EMB", [NEXP, ER], BF16))
        ONB = e(nc.sbuf_tensor("ONB", [128, 2048], F32))
        XRB = e(nc.sbuf_tensor("XRB", [128, 1024], F32))
        LGB = e(nc.sbuf_tensor("LGB", [128, 128], F32))
        RCB = e(nc.sbuf_tensor("RCB", [128, 512], F32))
        GTM = e(nc.sbuf_tensor("GTM", [128, 16], F32))
        SRW = e(nc.sbuf_tensor("SRW", [1, 3072], F32))
        ONF = e(nc.sbuf_tensor("ONF", [1, 128], F32))
        BIA = e(nc.sbuf_tensor("BIA", [128, 2], F32))

        sG = Sem(e(nc.semaphore("sG")))
        sS = Sem(e(nc.semaphore("sS")))
        sPE = Sem(e(nc.semaphore("sPE")))
        sACT = Sem(e(nc.semaphore("sACT")))
        sDVE = Sem(e(nc.semaphore("sDVE")))

        v_c = 0
        for dst, src in [(CSK[:, 0:T], d_cosk), (CSK[:, T:2 * T], d_sink),
                         (CSQ[:, 0:OWN], d_cosq), (CSQ[:, OWN:2 * OWN], d_sinq),
                         (MSK[:, :], d_masks), (IDN[:, :], d_ident),
                         (ONE[:, :], d_ones), (EMB[:, :], d_E)]:
            v_c = P.op("sync", "dma_start", dst, src, sem=sS, n=16)
        P.op("vector", "memset", BIA[:, 0:1], EXPB, sem=sDVE)
        P.op("vector", "memset", BIA[:, 1:2], 0.0, sem=sDVE)
        v_bia = P.op("vector", "memset", ONF[:, :], 1.0, sem=sDVE)
        P.wait("tensor", sS, v_c)
        P.wait("vector", sS, v_c)
        P.wait("scalar", sS, v_c)

        def ar1(c0, c1):
            return AR1[:, c0:c1]

        def ar2(c0, c1):
            return AR2[:, c0:c1]

        def dvs(method, *args, **kwargs):
            v = P.op("vector", method, *args, sem=sDVE, **kwargs)
            P.wait("vector", sDVE, v)
            return v

        v_pe_k = [[0] * 8 for _ in range(8)]
        v_rope_k = [[0] * 8 for _ in range(8)]
        if STOP_PHASE >= 1:
            with nc.psum_tensor("ps1A", [128, 512], F32) as psA, \
                 nc.psum_tensor("ps1B", [128, 512], F32) as psB:
                psK = [psA, psB]
                v_wk1 = P.op("gpsimd", "dma_start",
                             AR2[:, :].rearrange("p (dt o) -> p dt o", dt=32)
                             [:, :, 0:512],
                             d_wk.rearrange("(dt p) o -> p dt o", p=128)
                             [:, :, 0:512], sem=sG, n=16)
                v_wk2 = P.op("gpsimd", "dma_start",
                             AR2[:, :].rearrange("p (dt o) -> p dt o", dt=32)
                             [:, :, 512:1024],
                             d_wk.rearrange("(dt p) o -> p dt o", p=128)
                             [:, :, 512:1024], sem=sG, n=16)
                v_xt = [0] * 8
                v_sxtbf = [0] * 8
                v_skt = [[0, 0] for _ in range(8)]
                for tc in range(8):
                    sl = tc % 2
                    x0 = sl * 16384
                    if tc >= 2:
                        P.wait("gpsimd", sPE, v_pe_k[tc - 2][7])
                        P.wait("gpsimd", sS, v_sxtbf[tc - 2])
                    v_xt[tc] = P.op(
                        "gpsimd", "dma_start",
                        ar1(x0, x0 + 16384).rearrange("p (dt t) -> p dt t", dt=32),
                        d_xt.rearrange("(dt p) t -> p dt t", p=128)
                        [:, :, tc * 512:(tc + 1) * 512], sem=sG, n=16)
                    P.wait("sync", sG, v_xt[tc])
                    v_sxtbf[tc] = P.op(
                        "sync", "dma_start",
                        s_xtbf.rearrange("(dt p) t -> p dt t", p=128)
                        [:, :, tc * 512:(tc + 1) * 512],
                        ar1(x0, x0 + 16384).rearrange("p (dt t) -> p dt t", dt=32),
                        sem=sS, n=16)
                    P.wait("tensor", sG, max(v_xt[tc], v_wk1))
                    for h in range(8):
                        if h == 4 and tc == 0:
                            P.wait("tensor", sG, v_wk2)
                        gh = tc * 8 + h
                        if gh >= 2:
                            h2t, h2h = divmod(gh - 2, 8)
                            P.wait("tensor", sDVE, v_rope_k[h2t][h2h])
                        for dt in range(32):
                            sm = sPE if dt == 31 else None
                            v = P.op("tensor", "matmul", psK[h % 2][:, :],
                                     ar2(dt * 1024 + h * 128,
                                         dt * 1024 + (h + 1) * 128),
                                     ar1(x0 + dt * 512, x0 + (dt + 1) * 512),
                                     start=(dt == 0), stop=(dt == 31), sem=sm)
                            if sm:
                                v_pe_k[tc][h] = v
                        hb, half = divmod(h, 4)
                        kb = hb * 2048
                        P.wait("vector", sPE, v_pe_k[tc][h])
                        if half == 0:
                            prev = (tc * 2 + hb) - 2
                            if prev >= 0:
                                P.wait("vector", sS, v_skt[prev // 2][prev % 2])
                        ps = psK[h % 2]
                        cosb = CSK[:, tc * 512:(tc + 1) * 512]
                        sinb = CSK[:, T + tc * 512:T + (tc + 1) * 512]
                        t1 = ONB[0:64, 0:512]
                        t2 = ONB[0:64, 512:1024]
                        dvs("tensor_tensor", t1, ps[0:64, :], cosb, ALU.mult)
                        dvs("tensor_tensor", t2, ps[64:128, :], sinb, ALU.mult)
                        dvs("tensor_tensor",
                            KVB[0:64, kb + half * 512:kb + (half + 1) * 512],
                            t1, t2, ALU.subtract)
                        dvs("tensor_tensor", t1, ps[0:64, :], sinb, ALU.mult)
                        dvs("tensor_tensor", t2, ps[64:128, :], cosb, ALU.mult)
                        v_rope_k[tc][h] = dvs(
                            "tensor_tensor",
                            KVB[64:128, kb + half * 512:kb + (half + 1) * 512],
                            t1, t2, ALU.add)
                        if half == 3:
                            P.wait("sync", sDVE, v_rope_k[tc][h])
                            v_skt[tc][hb] = P.op(
                                "sync", "dma_start",
                                s_kt.rearrange("(h p) t -> p h t", p=128)
                                [:, hb * 4:hb * 4 + 4, tc * 512:(tc + 1) * 512],
                                KVB[:, kb:kb + 2048]
                                .rearrange("p (h t) -> p h t", h=4),
                                sem=sS, n=16)

        v_pe_v = [[0] * 4 for _ in range(8)]
        v_act_v = [[0] * 4 for _ in range(8)]
        v_sv = [[0] * 4 for _ in range(8)]
        if STOP_PHASE >= 2:
            with nc.psum_tensor("ps2A", [128, 1024], F32) as psA, \
                 nc.psum_tensor("ps2B", [128, 1024], F32) as psB:
                psV = [psA, psB]
                P.wait("gpsimd", sPE, v_pe_k[7][7])
                v_wv1 = P.op("gpsimd", "dma_start",
                             AR2[:, :].rearrange("p (dt o) -> p dt o", dt=32)
                             [:, :, 0:512],
                             d_wv.rearrange("(dt p) o -> p dt o", p=128)
                             [:, :, 0:512], sem=sG, n=16)
                v_wv = P.op("gpsimd", "dma_start",
                            AR2[:, :].rearrange("p (dt o) -> p dt o", dt=32)
                            [:, :, 512:1024],
                            d_wv.rearrange("(dt p) o -> p dt o", p=128)
                            [:, :, 512:1024], sem=sG, n=16)
                v_xbf = [0] * 8
                for tc in range(8):
                    sl = tc % 2
                    x0 = sl * 16384
                    if tc >= 2:
                        P.wait("sync", sPE, v_pe_v[tc - 2][3])
                    else:
                        P.wait("sync", sPE, v_pe_k[7][7])
                    P.wait("sync", sS, v_sxtbf[tc])
                    v_xbf[tc] = P.op(
                        "sync", "dma_start",
                        ar1(x0, x0 + 16384).rearrange("p (dt t) -> p dt t", dt=32),
                        s_xtbf.rearrange("(dt p) t -> p dt t", p=128)
                        [:, :, tc * 512:(tc + 1) * 512], sem=sS, n=16)
                    P.wait("tensor", sS, v_xbf[tc])
                    P.wait("tensor", sG, v_wv1 if tc == 0 else v_wv)
                    for tt in range(4):
                        gv = tc * 4 + tt
                        if gv >= 2:
                            p2t, p2h = divmod(gv - 2, 4)
                            P.wait("tensor", sACT, v_act_v[p2t][p2h])
                        for oc in range(2):
                            if tc == 0 and tt == 0 and oc == 1:
                                P.wait("tensor", sG, v_wv)
                            for dt in range(32):
                                sm = sPE if (dt == 31 and oc == 1) else None
                                v = P.op(
                                    "tensor", "matmul",
                                    psV[gv % 2][:, oc * 512:(oc + 1) * 512],
                                    ar1(x0 + dt * 512 + tt * 128,
                                        x0 + dt * 512 + (tt + 1) * 128),
                                    ar2(dt * 1024 + oc * 512,
                                        dt * 1024 + (oc + 1) * 512),
                                    start=(dt == 0), stop=(dt == 31), sem=sm)
                                if sm:
                                    v_pe_v[tc][tt] = v
                        P.wait("scalar", sPE, v_pe_v[tc][tt])
                        if gv >= 2:
                            p2t, p2h = divmod(gv - 2, 4)
                            P.wait("scalar", sS, v_sv[p2t][p2h])
                        v_act_v[tc][tt] = P.op(
                            "scalar", "copy",
                            KVB[:, (gv % 2) * 1024:(gv % 2 + 1) * 1024],
                            psV[gv % 2][:, :], sem=sACT)
                        P.wait("sync", sACT, v_act_v[tc][tt])
                        r0 = tc * 512 + tt * 128
                        v_sv[tc][tt] = P.op(
                            "sync", "dma_start", s_v[r0:r0 + 128, :],
                            KVB[:, (gv % 2) * 1024:(gv % 2 + 1) * 1024],
                            sem=sS, n=16)

        v_pe_q = [0] * 32
        v_rope_q = [0] * 32
        v_sqt = [0] * 8
        if STOP_PHASE >= 3:
            with nc.psum_tensor("ps3A", [128, 1024], F32) as psA, \
                 nc.psum_tensor("ps3B", [128, 1024], F32) as psB:
                psQ = [psA, psB]
                P.wait("gpsimd", sPE, v_pe_v[7][3])

                def _xtq_dma(hf):
                    return P.op(
                        "gpsimd", "dma_start",
                        AR1[:, :].rearrange("p (dt t) -> p dt t", dt=32)
                        [:, :, hf * 512:(hf + 1) * 512],
                        d_xtq.rearrange("(dt p) t -> p dt t", p=128)
                        [:, :, hf * 512:(hf + 1) * 512], sem=sG, n=16)

                v_xtqh = [_xtq_dma(0), 0]
                v_wq = [0] * 32
                for ot in range(32):
                    sl = ot % 2
                    w0 = sl * 4096
                    if ot >= 2:
                        P.wait("gpsimd", sPE, v_pe_q[ot - 2])
                    else:
                        P.wait("gpsimd", sACT, v_act_v[7][3])
                    v_wq[ot] = P.op(
                        "gpsimd", "dma_start",
                        ar2(w0, w0 + 4096).rearrange("p (dt o) -> p dt o", dt=32),
                        d_wq.rearrange("(dt p) o -> p dt o", p=128)
                        [:, :, ot * 128:(ot + 1) * 128], sem=sG, n=16)
                    if ot == 0:
                        v_xtqh[1] = _xtq_dma(1)
                    P.wait("tensor", sG, max(v_wq[ot], v_xtqh[0]))
                    if ot >= 2:
                        P.wait("tensor", sDVE, v_rope_q[ot - 2])
                    for half in range(2):
                        if ot == 0 and half == 1:
                            P.wait("tensor", sG, v_xtqh[1])
                        for dt in range(32):
                            sm = sPE if (dt == 31 and half == 1) else None
                            v = P.op("tensor", "matmul",
                                     psQ[sl][:, half * 512:(half + 1) * 512],
                                     ar2(w0 + dt * 128, w0 + (dt + 1) * 128),
                                     ar1(dt * 1024 + half * 512,
                                         dt * 1024 + (half + 1) * 512),
                                     start=(dt == 0), stop=(dt == 31), sem=sm)
                            if sm:
                                v_pe_q[ot] = v
                    P.wait("vector", sPE, v_pe_q[ot])
                    if ot >= 4:
                        P.wait("vector", sS, v_sqt[(ot - 4) // 4])
                    ps = psQ[sl]
                    cosb = CSQ[:, 0:OWN]
                    sinb = CSQ[:, OWN:2 * OWN]
                    q0 = (ot % 4) * 1024
                    t1 = ONB[0:64, 0:1024]
                    t2 = ONB[0:64, 1024:2048]
                    dvs("tensor_tensor", t1, ps[0:64, :], cosb, ALU.mult)
                    dvs("tensor_tensor", t2, ps[64:128, :], sinb, ALU.mult)
                    dvs("tensor_tensor", HPB[0:64, q0:q0 + 1024],
                        t1, t2, ALU.subtract)
                    dvs("tensor_tensor", t1, ps[0:64, :], sinb, ALU.mult)
                    dvs("tensor_tensor", t2, ps[64:128, :], cosb, ALU.mult)
                    v_rope_q[ot] = dvs("tensor_tensor",
                                       HPB[64:128, q0:q0 + 1024], t1, t2,
                                       ALU.add)
                    if ot % 4 == 3:
                        P.wait("sync", sDVE, v_rope_q[ot])
                        v_sqt[ot // 4] = P.op(
                            "sync", "dma_start",
                            s_qt.rearrange("(q p) t -> p q t", p=128)
                            [:, ot - 3:ot + 1, :],
                            HPB[:, :].rearrange("p (q t) -> p q t", q=4),
                            sem=sS, n=16)

        v_onorm = {}
        v_head_pe = [0] * 8
        if STOP_PHASE >= 4:
            with nc.psum_tensor("ps4A", [128, 1024], F32) as psA, \
                 nc.psum_tensor("ps4B", [128, 1024], F32) as psB, \
                 nc.psum_tensor("ps4C", [128, 1024], F32) as psC, \
                 nc.psum_tensor("ps4D1", [128, 512], F32) as psD1, \
                 nc.psum_tensor("ps4D2", [128, 512], F32) as psD2:
                psS = [psA, psB, psC]
                psDA = [psD1, psD2]
                v_head_ld = [0] * 8
                gi = 0
                v_sgrp = {}
                v_exp = {}
                v_mask = {}
                v_av = {}
                for h in range(8):
                    sl = h % 2
                    kt0 = sl * 4096
                    vt0 = 8192 + sl * 4096
                    qt0 = 16384 + sl * 4096
                    if h >= 2:
                        P.wait("sync", sPE, v_head_pe[h - 2])
                    elif h == 0:
                        P.wait("sync", sDVE, v_rope_q[31])
                        P.wait("sync", sS, v_skt[7][1])
                        P.wait("sync", sS, v_sv[7][3])
                    P.wait("sync", sS, v_sqt[h])
                    P.op("sync", "dma_start", ar1(kt0, kt0 + 4096),
                         s_kt[h * 128:(h + 1) * 128, :], sem=sS, n=16)
                    P.op("sync", "dma_start",
                         ar1(vt0, vt0 + 4096).rearrange("p (tt o) -> p tt o", tt=32),
                         s_v.rearrange("(tt p) o -> p tt o", p=128)
                         [:, :, h * 128:(h + 1) * 128], sem=sS, n=16)
                    v_head_ld[h] = P.op(
                        "sync", "dma_start",
                        ar1(qt0, qt0 + 4096).rearrange("p (g t) -> p g t", g=4),
                        s_qt.rearrange("(q p) t -> p q t", p=128)
                        [:, 4 * h:4 * h + 4, :], sem=sS, n=16)
                    P.wait("tensor", sS, v_head_ld[h])
                    for g in range(4):
                        for c in range(4):
                            ntk = 8 * c + 8
                            ngrp = ntk // 4
                            ci = (h * 4 + g) * 4 + c
                            rhs = ar1(qt0 + g * 1024 + c * 256,
                                      qt0 + g * 1024 + (c + 1) * 256)
                            for grp in range(ngrp):
                                if gi >= 3:
                                    P.wait("tensor", sACT, v_exp[gi - 3])
                                for i in range(4):
                                    k = 4 * grp + i
                                    sm = sPE if i == 3 else None
                                    v = P.op("tensor", "matmul",
                                             psS[gi % 3][:, i * 256:(i + 1) * 256],
                                             ar1(kt0 + k * 128, kt0 + (k + 1) * 128),
                                             rhs, start=(i % 2 == 0), stop=True,
                                             skip_group_check=True, sem=sm)
                                    if sm:
                                        v_sgrp[gi] = v
                                if gi == 0:
                                    P.wait("scalar", sS, v_sv[7][3])
                                P.wait("scalar", sPE, v_sgrp[gi])
                                if gi >= 4:
                                    P.wait("scalar", sPE, v_av[gi - 4])
                                p0 = (gi % 4) * 1024
                                v_exp[gi] = P.op("scalar", "activation",
                                                 KVB[:, p0:p0 + 1024],
                                                 psS[gi % 3][:, :], AF.Exp,
                                                 bias=BIA[:, 0:1], scale=RS,
                                                 sem=sACT)
                                diag = grp >= ngrp - 2
                                if diag:
                                    r0 = 4 * grp - (ntk - 8)
                                    P.wait("vector", sACT, v_exp[gi])
                                    v_mask[gi] = P.op(
                                        "vector", "tensor_tensor",
                                        KVB[:, p0:p0 + 1024],
                                        KVB[:, p0:p0 + 1024],
                                        MSK[:, r0 * 256:r0 * 256 + 1024],
                                        ALU.mult, sem=sDVE)
                                if diag:
                                    P.wait("tensor", sDVE, v_mask[gi])
                                else:
                                    P.wait("tensor", sACT, v_exp[gi])
                                if grp == 0 and ci >= 2:
                                    P.wait("tensor", sDVE, v_onorm[ci - 2])
                                for i in range(4):
                                    k = 4 * grp + i
                                    P.op("tensor", "matmul",
                                         psDA[c % 2][:, 0:256], ONE[:, :],
                                         KVB[:, p0 + i * 256:p0 + (i + 1) * 256],
                                         start=(k == 0), stop=(k == ntk - 1),
                                         skip_group_check=True)
                                for i in range(4):
                                    k = 4 * grp + i
                                    sm = sPE if i == 3 else None
                                    v = P.op("tensor", "matmul",
                                             psDA[c % 2][:, 256:512],
                                             ar1(vt0 + k * 128, vt0 + (k + 1) * 128),
                                             KVB[:, p0 + i * 256:p0 + (i + 1) * 256],
                                             start=False, stop=(k == ntk - 1),
                                             skip_group_check=True, sem=sm)
                                    if sm:
                                        v_av[gi] = v
                                gi += 1
                            P.wait("vector", sPE, v_av[gi - 1])
                            if ci == 0:
                                P.wait("vector", sPE, v_pe_q[31])
                            rc = RCB[:, (c % 2) * 256:(c % 2 + 1) * 256]
                            dvs("reciprocal", rc, psDA[c % 2][:, 0:256])
                            hq = 4 * h + g
                            v_onorm[ci] = dvs(
                                "tensor_tensor",
                                ar2(hq * 1024 + c * 256, hq * 1024 + (c + 1) * 256),
                                psDA[c % 2][:, 256:512], rc, ALU.mult)
                    v_head_pe[h] = v_av[gi - 1]

        v_pe_wo = [0] * 32
        v_pe_st = [0] * 32
        v_mrb = 0
        if STOP_PHASE >= 5:
            st56 = ExitStack()
            psA = st56.enter_context(nc.psum_tensor("ps5A", [128, 1024], F32))
            psB = st56.enter_context(nc.psum_tensor("ps5B", [128, 1024], F32))
            psSx = st56.enter_context(nc.psum_tensor("ps5Sx", [128, 1024], F32))
            psSq = st56.enter_context(nc.psum_tensor("ps5Sq", [128, 1024], F32))
            if True:
                psW = [psA, psB]
                v_wo = [0] * 32
                v_xr = [0] * 32
                v_hp = [0] * 32
                v_shp = [0] * 8
                for ot in range(32):
                    sl = ot % 2
                    w0 = sl * 4096
                    if ot >= 2:
                        P.wait("gpsimd", sPE, v_pe_wo[ot - 2])
                    else:
                        P.wait("gpsimd", sPE, v_head_pe[7])
                    v_wo[ot] = P.op(
                        "gpsimd", "dma_start",
                        ar1(w0, w0 + 4096).rearrange("p (dt o) -> p dt o", dt=32),
                        d_wo.rearrange("(dt p) o -> p dt o", p=128)
                        [:, :, ot * 128:(ot + 1) * 128], sem=sG, n=16)
                    if ot >= 1:
                        P.wait("sync", sDVE, v_hp[ot - 1])
                    v_xr[ot] = P.op("sync", "dma_start", XRB[:, 0:1024],
                                    d_xtq[ot * 128:(ot + 1) * 128, :],
                                    sem=sS, n=16)
                    P.wait("tensor", sG, v_wo[ot])
                    if ot == 0:
                        P.wait("tensor", sDVE, v_onorm[127])
                    if ot >= 2:
                        P.wait("tensor", sDVE, v_hp[ot - 2])
                    for half in range(2):
                        for dt in range(32):
                            sm = sPE if (dt == 31 and half == 1) else None
                            v = P.op("tensor", "matmul",
                                     psW[sl][:, half * 512:(half + 1) * 512],
                                     ar1(w0 + dt * 128, w0 + (dt + 1) * 128),
                                     ar2(dt * 1024 + half * 512,
                                         dt * 1024 + (half + 1) * 512),
                                     start=(dt == 0), stop=(dt == 31), sem=sm)
                            if sm:
                                v_pe_wo[ot] = v
                    P.wait("vector", sPE, v_pe_wo[ot])
                    P.wait("vector", sS, v_xr[ot])
                    if ot == 0:
                        P.wait("vector", sS, v_sqt[7])
                    if ot >= 1:
                        P.wait("vector", sPE, v_pe_st[ot - 1])
                    if ot >= 4:
                        P.wait("vector", sS, v_shp[(ot - 4) // 4])
                    hp = HPB[:, (ot % 4) * 1024:(ot % 4 + 1) * 1024]
                    dvs("tensor_tensor", hp, psW[sl][:, :],
                        XRB[:, 0:1024], ALU.add)
                    v_hp[ot] = dvs("tensor_tensor", SQB[:, :], hp, hp, ALU.mult)
                    P.wait("tensor", sDVE, v_hp[ot])
                    for half in range(2):
                        P.op("tensor", "matmul",
                             psSx[0:1, half * 512:(half + 1) * 512],
                             ONE[:, 0:1], hp[:, half * 512:(half + 1) * 512],
                             start=(ot == 0), stop=(ot == 31),
                             skip_group_check=True)
                        sm = sPE if half == 1 else None
                        v = P.op("tensor", "matmul",
                                 psSq[0:1, half * 512:(half + 1) * 512],
                                 ONE[:, 0:1], SQB[:, half * 512:(half + 1) * 512],
                                 start=(ot == 0), stop=(ot == 31),
                                 skip_group_check=True, sem=sm)
                        if sm:
                            v_pe_st[ot] = v
                    if ot % 4 == 3:
                        P.wait("sync", sDVE, v_hp[ot])
                        v_shp[ot // 4] = P.op(
                            "sync", "dma_start",
                            s_hpre.rearrange("(o p) t -> p o t", p=128)
                            [:, ot - 3:ot + 1, :],
                            HPB[:, :].rearrange("p (o t) -> p o t", o=4),
                            sem=sS, n=16)
                v_mrb = _ln_finalize(P, psSx, psSq, v_pe_st[31], SRW,
                                     BIA, ONF, sPE, sACT, sDVE, v_bia)

        GW0 = 0
        AT0 = 1024
        BL0 = 8192
        GE0 = 16384
        HS0 = 18432
        GN0 = 20480
        GT0 = 22528
        WS0 = 24576

        v_h1 = [0] * 32
        v_ffnw = 0
        if STOP_PHASE >= 6:
            P.wait("gpsimd", sPE, v_pe_wo[31])
            P.op("gpsimd", "dma_start",
                 ar2(GW0, GW0 + 896).rearrange("p (dt e) -> p dt e", dt=32),
                 d_gwT.rearrange("(dt p) e -> p dt e", p=128), sem=sG, n=16)
            P.op("gpsimd", "dma_start",
                 ar2(AT0, AT0 + 7168).rearrange("p (dt r) -> p dt r", dt=32),
                 d_aT.rearrange("(dt p) r -> p dt r", p=128), sem=sG, n=16)
            P.op("gpsimd", "dma_start", ar2(BL0, BL0 + 4096),
                 d_bl[0:128, :], sem=sG, n=16)
            v_ffnw = P.op("gpsimd", "dma_start",
                          AR2[0:96, BL0 + 4096:BL0 + 8192],
                          d_bl[128:224, :], sem=sG, n=16)
            v_hp6 = [0] * 32
            for i in range(34):
                if i < 32:
                    if i < 4:
                        P.wait("sync", sPE, v_pe_st[31])
                    else:
                        P.wait("sync", sDVE, v_h1[i - 4])
                    P.wait("sync", sS, v_shp[i // 4])
                    v_hp6[i] = P.op("sync", "dma_start",
                                    HPB[:, (i % 4) * 1024:(i % 4 + 1) * 1024],
                                    s_hpre[i * 128:(i + 1) * 128, :],
                                    sem=sS, n=16)
                if i >= 2:
                    dt = i - 2
                    sl = dt % 4
                    P.wait("vector", sS, v_hp6[min(dt + 2, 31)])
                    if dt == 0:
                        P.wait("vector", sPE, v_pe_wo[31])
                        P.wait("vector", sPE, v_mrb)
                    hp = HPB[:, sl * 1024:(sl + 1) * 1024]
                    dvs("tensor_tensor", LTM[:, :], hp, psSx[:, 0:1024],
                        ALU.subtract)
                    v_h1[dt] = dvs("tensor_tensor",
                                   ar1(dt * 1024, (dt + 1) * 1024),
                                   LTM[:, :], psSq[:, 0:1024], ALU.mult)
            if DEBUG:
                P.wait("sync", sDVE, v_h1[31])
                for dt in range(32):
                    vrd = P.op("sync", "dma_start", HPB[:, 0:1024],
                               s_hpre[dt * 128:(dt + 1) * 128, :],
                               sem=sS, n=16)
                    P.wait("vector", sS, vrd)
                    dvs("tensor_tensor", LTM[:, :], HPB[:, 0:1024],
                        psSx[:, 0:1024], ALU.subtract)
                    v_dbg = dvs("tensor_tensor", SQB[:, :],
                                LTM[:, :], psSq[:, 0:1024], ALU.mult)
                    P.wait("sync", sDVE, v_dbg)
                    vwr = P.op("sync", "dma_start",
                               s_h1b[dt * 128:(dt + 1) * 128, :], SQB[:, :],
                               sem=sS, n=16)
                    P.wait("vector", sS, vwr)
                v1 = dvs("tensor_tensor", LTM[:, :], psSx[:, 0:1024],
                         LTM[:, :], ALU.bypass)
                P.wait("sync", sDVE, v1)
                vda = P.op("sync", "dma_start", s_dbg3[:, 0:1024], LTM[:, :],
                           sem=sS, n=16)
                P.wait("vector", sS, vda)
                v2 = dvs("tensor_tensor", LTM[:, :], psSq[:, 0:1024],
                         LTM[:, :], ALU.bypass)
                P.wait("sync", sDVE, v2)
                P.op("sync", "dma_start", s_dbg3[:, 1024:2048], LTM[:, :],
                     sem=sS, n=16)
            st56.close()

        if DEBUG and STOP_PHASE >= 6:
            P.wait("sync", sDVE, v_h1[31])
            P.op("sync", "dma_start",
                 s_h1.rearrange("(dt p) t -> p dt t", p=128),
                 AR1[:, :].rearrange("p (dt t) -> p dt t", dt=32),
                 sem=sS, n=16)

        v_hsc = [0, 0]
        v_pe_lora = 0
        if STOP_PHASE >= 7:
            with nc.psum_tensor("ps7L", [128, 1024], F32) as psL, \
                 nc.psum_tensor("ps7T", [128, 2048], BF16) as psT, \
                 nc.psum_tensor("ps7E1", [128, 1024], F32) as psE1, \
                 nc.psum_tensor("ps7E2", [128, 1024], F32) as psE2:
                v_lg = [0] * 8
                v_g = [0] * 8
                v_gt = [0] * 8
                v_gte = [0] * 8
                P.wait("tensor", sDVE, v_h1[31])
                P.wait("tensor", sG, v_ffnw)
                for tt in range(8):
                    lb = (tt % 2) * 512
                    if tt >= 2:
                        P.wait("tensor", sDVE, v_g[tt - 2])
                    for dt in range(32):
                        sm = sPE if dt == 31 else None
                        v = P.op("tensor", "matmul", psL[:, lb:lb + NEXP],
                                 ar1(dt * 1024 + tt * 128,
                                     dt * 1024 + (tt + 1) * 128),
                                 ar2(GW0 + dt * 28, GW0 + (dt + 1) * 28),
                                 start=(dt == 0), stop=(dt == 31), sem=sm)
                        if sm:
                            v_lg[tt] = v
                    P.wait("vector", sPE, v_lg[tt])

                    def dss(method, *args, **kwargs):
                        v = P.op("vector", method, *args, sem=sDVE, **kwargs)
                        P.wait("vector", sDVE, v)
                        return v

                    lg = LGB[:, 0:NEXP]
                    m1 = GTM[:, 0:1]
                    m1n = GTM[:, 1:2]
                    cnt = GTM[:, 2:3]
                    m2 = GTM[:, 3:4]
                    is2 = GTM[:, 4:5]
                    thr = GTM[:, 5:6]
                    zs = GTM[:, 6:7]
                    rz = GTM[:, 7:8]
                    ge1 = LGB[:, 32:60]
                    wv_ = LGB[:, 64:92]
                    kp = LGB[:, 96:124]
                    dss("tensor_tensor", lg, psL[:, lb:lb + NEXP], lg,
                        ALU.bypass)
                    dss("tensor_reduce", m1, lg, AX.X, ALU.max)
                    v_pre = dss("tensor_scalar", m1n, m1, -1.0, None, ALU.mult)
                    dss("tensor_scalar", ge1, lg, m1, None,
                        ALU.is_ge, op1=ALU.add, accum_out=cnt)
                    P.wait("scalar", sDVE, v_pre)
                    v_w = P.op("scalar", "activation", wv_, lg, AF.Exp,
                               bias=m1n, scale=1.0, sem=sACT)
                    dss("scalar_tensor_tensor", kp, ge1, -1e30, lg,
                        ALU.mult, ALU.add)
                    dss("tensor_reduce", m2, kp, AX.X, ALU.max)
                    dss("tensor_scalar", is2, cnt, 1.5, None, ALU.is_ge)
                    dss("tensor_tensor", thr, m1, m2, ALU.subtract)
                    dss("tensor_tensor", thr, is2, thr, ALU.mult)
                    dss("tensor_tensor", thr, thr, m2, ALU.add)
                    dss("tensor_scalar", kp, psL[:, lb:lb + NEXP],
                        thr, None, ALU.is_ge)
                    P.wait("vector", sACT, v_w)
                    dss("tensor_tensor", wv_, wv_, kp, ALU.mult)
                    dss("tensor_reduce", zs, wv_, AX.X, ALU.add)
                    dss("reciprocal", rz, zs)
                    v_g[tt] = dss("tensor_scalar",
                                  ar2(GN0 + tt * 28, GN0 + (tt + 1) * 28),
                                  wv_, rz, None, ALU.mult)
                    if DEBUG:
                        P.wait("sync", sDVE, v_g[tt])
                        vd = P.op("sync", "dma_start",
                                  s_dbg2[:, tt * 16:(tt + 1) * 16],
                                  GTM[:, 0:16], sem=sS, n=16)
                        if tt == 0:
                            vd = P.op("sync", "dma_start", s_dbg2[:, 128:256],
                                      LGB[:, :], sem=sS, n=16)
                        P.wait("vector", sS, vd)
                    P.wait("tensor", sDVE, v_g[tt])
                    if tt >= 2:
                        P.wait("tensor", sDVE, v_gte[tt - 2])
                    pT = psT[0:28, (tt % 2) * 1024:(tt % 2) * 1024 + 128]
                    v_gt[tt] = P.op("tensor", "transpose", pT,
                                    ar2(GN0 + tt * 28, GN0 + (tt + 1) * 28),
                                    IDN[:, :], sem=sPE)
                    P.wait("vector", sPE, v_gt[tt])
                    gslc = AR2[0:28, GT0 + tt * 128:GT0 + (tt + 1) * 128]
                    v_gte[tt] = dss("tensor_tensor", gslc, pT[0:28, :], gslc,
                                    ALU.bypass)
                P.wait("tensor", sDVE, v_gte[7])
                psEx = [psE1, psE2]
                v_ex = [0, 0]
                for ert in range(2):
                    m = 128 if ert == 0 else 96
                    for half in range(2):
                        sm = sPE if half == 1 else None
                        v = P.op("tensor", "matmul",
                                 psEx[ert][0:m, half * 512:(half + 1) * 512],
                                 EMB[:, ert * 128:ert * 128 + m],
                                 AR2[0:28, GT0 + half * 512:GT0 + (half + 1) * 512],
                                 start=True, stop=True, sem=sm)
                        if sm:
                            v_ex[ert] = v
                v_gee = [0, 0]
                for ert in range(2):
                    m = 128 if ert == 0 else 96
                    P.wait("vector", sPE, v_ex[ert])
                    gsl = AR2[0:m, GE0 + ert * 1024:GE0 + (ert + 1) * 1024]
                    v_gee[ert] = P.op("vector", "tensor_tensor", gsl,
                                      psEx[ert][0:m, 0:1024], gsl,
                                      ALU.bypass, sem=sDVE)
                    P.wait("vector", sDVE, v_gee[ert])
                for ert in range(2):
                    m = 128 if ert == 0 else 96
                    P.wait("tensor", sDVE, v_gee[ert])
                    vh = 0
                    for half in range(2):
                        for dt in range(32):
                            sm = sPE if (dt == 31 and half == 1) else None
                            v = P.op("tensor", "matmul",
                                     psEx[ert][0:m, half * 512:(half + 1) * 512],
                                     ar2(AT0 + dt * 224 + ert * 128,
                                         AT0 + dt * 224 + ert * 128 + m),
                                     ar1(dt * 1024 + half * 512,
                                         dt * 1024 + (half + 1) * 512),
                                     start=(dt == 0), stop=(dt == 31), sem=sm)
                            if sm:
                                vh = v
                    v_pe_lora = vh
                    P.wait("vector", sPE, vh)
                    v_hsc[ert] = P.op("vector", "tensor_tensor",
                                      AR2[0:m, HS0 + ert * 1024:
                                          HS0 + (ert + 1) * 1024],
                                      psEx[ert][0:m, 0:1024],
                                      AR2[0:m, GE0 + ert * 1024:
                                          GE0 + (ert + 1) * 1024],
                                      ALU.mult, sem=sDVE)

        if DEBUG and STOP_PHASE >= 7:
            P.wait("sync", sDVE, v_hsc[1])
            P.op("sync", "dma_start", s_dbg[:, 0:224], ar2(GN0, GN0 + 224),
                 sem=sS, n=16)
            P.op("sync", "dma_start", s_dbg[:, 1024:3072],
                 ar2(HS0, HS0 + 2048), sem=sS, n=16)
            P.op("sync", "dma_start", s_dbg[0:28, 3072:4096],
                 AR2[0:28, GT0:GT0 + 1024], sem=sS, n=16)

        v_pe_f = [0] * 32
        v_pe_st8 = [0] * 32
        v_mrb2 = 0
        if STOP_PHASE >= 8:
            st89 = ExitStack()
            psSx = st89.enter_context(nc.psum_tensor("ps8Sx", [128, 1024], F32))
            psSq = st89.enter_context(nc.psum_tensor("ps8Sq", [128, 1024], F32))
            st8i = ExitStack()
            psA = st8i.enter_context(nc.psum_tensor("ps8A", [128, 1024], F32))
            psB = st8i.enter_context(nc.psum_tensor("ps8B", [128, 1024], F32))
            if True:
                psF = [psA, psB]
                v_bw = [0] * 32
                v_hp8 = [0] * 32
                v_shp8 = [0] * 8
                for ot in range(32):
                    sl = ot % 2
                    w0 = WS0 + sl * 4096
                    if ot >= 2:
                        P.wait("gpsimd", sPE, v_pe_f[ot - 2])
                    v_bw[ot] = P.op(
                        "gpsimd", "dma_start",
                        ar2(w0, w0 + 4096).rearrange("p (dt o) -> p dt o", dt=32),
                        d_bwT.rearrange("(dt p) o -> p dt o", p=128)
                        [:, :, ot * 128:(ot + 1) * 128], sem=sG, n=16)
                    P.wait("tensor", sG, v_bw[ot])
                    if ot == 0:
                        P.wait("tensor", sDVE, v_hsc[1])
                    if ot >= 2:
                        P.wait("tensor", sDVE, v_hp8[ot - 2])
                    for half in range(2):
                        for dt in range(32):
                            P.op("tensor", "matmul",
                                 psF[sl][:, half * 512:(half + 1) * 512],
                                 ar2(w0 + dt * 128, w0 + (dt + 1) * 128),
                                 ar1(dt * 1024 + half * 512,
                                     dt * 1024 + (half + 1) * 512),
                                 start=(dt == 0), stop=False)
                        for ert in range(2):
                            kk = 128 if ert == 0 else 96
                            sm = sPE if (ert == 1 and half == 1) else None
                            v = P.op("tensor", "matmul",
                                     psF[sl][:, half * 512:(half + 1) * 512],
                                     AR2[0:kk, BL0 + ert * 4096 + ot * 128:
                                         BL0 + ert * 4096 + (ot + 1) * 128],
                                     AR2[0:kk, HS0 + ert * 1024 + half * 512:
                                         HS0 + ert * 1024 + (half + 1) * 512],
                                     start=False, stop=(ert == 1),
                                     skip_group_check=True, sem=sm)
                            if sm:
                                v_pe_f[ot] = v
                    P.wait("vector", sPE, v_pe_f[ot])
                    if ot == 0:
                        P.wait("vector", sS, v_shp[7] if STOP_PHASE >= 5 else 0)
                    if ot >= 1:
                        P.wait("vector", sPE, v_pe_st8[ot - 1])
                    if ot >= 4:
                        P.wait("vector", sS, v_shp8[(ot - 4) // 4])
                    hp = HPB[:, (ot % 4) * 1024:(ot % 4 + 1) * 1024]
                    dvs("tensor_tensor", hp, psF[sl][:, :],
                        ar1(ot * 1024, (ot + 1) * 1024), ALU.add)
                    v_hp8[ot] = dvs("tensor_tensor", SQB[:, :], hp, hp,
                                    ALU.mult)
                    P.wait("tensor", sDVE, v_hp8[ot])
                    for half in range(2):
                        P.op("tensor", "matmul",
                             psSx[0:1, half * 512:(half + 1) * 512],
                             ONE[:, 0:1], hp[:, half * 512:(half + 1) * 512],
                             start=(ot == 0), stop=(ot == 31),
                             skip_group_check=True)
                        sm = sPE if half == 1 else None
                        v = P.op("tensor", "matmul",
                                 psSq[0:1, half * 512:(half + 1) * 512],
                                 ONE[:, 0:1], SQB[:, half * 512:(half + 1) * 512],
                                 start=(ot == 0), stop=(ot == 31),
                                 skip_group_check=True, sem=sm)
                        if sm:
                            v_pe_st8[ot] = v
                    if ot % 4 == 3:
                        P.wait("sync", sDVE, v_hp8[ot])
                        v_shp8[ot // 4] = P.op(
                            "sync", "dma_start",
                            s_h2pre.rearrange("(o p) t -> p o t", p=128)
                            [:, ot - 3:ot + 1, :],
                            HPB[:, :].rearrange("p (o t) -> p o t", o=4),
                            sem=sS, n=16)
                st8i.close()
                v_mrb2 = _ln_finalize(P, psSx, psSq, v_pe_st8[31], SRW,
                                      BIA, ONF, sPE, sACT, sDVE, v_bia)

        if STOP_PHASE >= 9:
            with nc.psum_tensor("ps9T", [128, 2048], BF16) as psT8:
                v_h2 = [0] * 32
                v_hp9 = [0] * 32
                for i in range(34):
                    if i < 32:
                        if i < 4:
                            P.wait("sync", sPE, v_pe_st8[31])
                        else:
                            P.wait("sync", sDVE, v_h2[i - 4])
                        P.wait("sync", sS, v_shp8[i // 4])
                        v_hp9[i] = P.op(
                            "sync", "dma_start",
                            HPB[:, (i % 4) * 1024:(i % 4 + 1) * 1024],
                            s_h2pre[i * 128:(i + 1) * 128, :], sem=sS, n=16)
                    if i >= 2:
                        dt = i - 2
                        sl = dt % 4
                        P.wait("vector", sS, v_hp9[min(dt + 2, 31)])
                        if dt == 0:
                            P.wait("vector", sPE, v_pe_f[31])
                            P.wait("vector", sPE, v_mrb2)
                        hp = HPB[:, sl * 1024:(sl + 1) * 1024]
                        dvs("tensor_tensor", LTM[:, :], hp,
                            psSx[:, 0:1024], ALU.subtract)
                        v_h2[dt] = dvs("tensor_tensor",
                                       ar2(dt * 1024, (dt + 1) * 1024),
                                       LTM[:, :], psSq[:, 0:1024], ALU.mult)
                v_tr = {}
                v_trc = {}
                v_outd = [[0, 0] for _ in range(8)]
                for tt in range(8):
                    for hf in range(2):
                        for dd in range(16):
                            dt = hf * 16 + dd
                            ti = tt * 32 + dt
                            hi = tt * 2 + hf
                            if ti == 0:
                                P.wait("tensor", sDVE, v_h2[31])
                            if ti >= 2:
                                P.wait("tensor", sDVE, v_trc[ti - 2])
                            pT = psT8[:, (ti % 2) * 1024:(ti % 2) * 1024 + 128]
                            v_tr[ti] = P.op(
                                "tensor", "transpose", pT,
                                ar2(dt * 1024 + tt * 128,
                                    dt * 1024 + (tt + 1) * 128),
                                IDN[:, :], sem=sPE)
                            P.wait("vector", sPE, v_tr[ti])
                            if dd == 0 and hi >= 1:
                                ph, pf = divmod(hi - 1, 2)
                                P.wait("vector", sS, v_outd[ph][pf])
                            osl = ONB[:, dd * 128:(dd + 1) * 128]
                            v_trc[ti] = P.op(
                                "vector", "tensor_tensor", osl, pT, osl,
                                ALU.bypass, sem=sDVE)
                        P.wait("sync", sDVE, v_trc[tt * 32 + hf * 16 + 15])
                        v_outd[tt][hf] = P.op(
                            "sync", "dma_start",
                            d_out[tt * 128:(tt + 1) * 128,
                                  hf * 2048:(hf + 1) * 2048],
                            ONB[:, :], sem=sS, n=16)
                P.wait("sync", sS, v_outd[7][1])
            st89.close()

        P.wait("gpsimd", sG, sG.v)
        P.wait("sync", sS, sS.v)

        with nc.Block() as blk:
            @blk.tensor
            def _(E):
                _replay(E, P.streams["tensor"])

            @blk.vector
            def _(E):
                _replay(E, P.streams["vector"])

            @blk.scalar
            def _(E):
                _replay(E, P.streams["scalar"])

            @blk.gpsimd
            def _(E):
                _replay(E, P.streams["gpsimd"])

            @blk.sync
            def _(E):
                _replay(E, P.streams["sync"])

    return nc


def _ln_finalize(P, psSx, psSq, v_stats_done, SRW, BIA, ONF,
                 sPE, sACT, sDVE, v_bia):
    m = SRW[0:1, 0:1024]
    s2 = SRW[0:1, 1024:2048]
    s3 = SRW[0:1, 2048:3072]

    def dss(method, *args, **kwargs):
        v = P.op("vector", method, *args, sem=sDVE, **kwargs)
        P.wait("vector", sDVE, v)
        return v

    P.wait("vector", sPE, v_stats_done)
    dss("tensor_scalar", m, psSx[0:1, 0:1024], 1.0 / 4096.0, None, ALU.mult)
    v_rd = dss("tensor_scalar", s2, psSq[0:1, 0:1024], 1.0 / 4096.0, None,
               ALU.mult)
    dss("tensor_tensor", s3, m, m, ALU.mult)
    dss("tensor_tensor", s2, s2, s3, ALU.subtract)
    v1 = dss("tensor_scalar", s2, s2, 1e-5, None, ALU.add)
    P.wait("scalar", sDVE, max(v1, v_bia))
    va = P.op("scalar", "activation", s3, s2, AF.Ln, bias=BIA[0:1, 1:2],
              scale=1.0, sem=sACT)
    P.wait("scalar", sACT, va)
    v2 = P.op("scalar", "activation", s3, s3, AF.Exp, bias=BIA[0:1, 1:2],
              scale=-0.5, sem=sACT)
    P.wait("tensor", sDVE, v_rd)
    P.wait("tensor", sACT, v2)
    vb = 0
    for q, (src, ps) in enumerate([(m, psSx), (s3, psSq)]):
        for half in range(2):
            sm = sPE if (q == 1 and half == 1) else None
            v = P.op("tensor", "matmul", ps[:, half * 512:(half + 1) * 512],
                     ONF[0:1, :],
                     src[0:1, half * 512:(half + 1) * 512],
                     start=True, stop=True, sem=sm)
            if sm:
                vb = v
    return vb



def _prep_host(inputs):
    X = np.asarray(inputs["L0_attn_X"], dtype=np.float32)
    wq = np.ascontiguousarray(np.asarray(inputs["L0_attn_Wq"], np.float32))
    wk = np.ascontiguousarray(np.asarray(inputs["L0_attn_Wk"], np.float32))
    wv = np.ascontiguousarray(np.asarray(inputs["L0_attn_Wv"], np.float32))
    wo = np.ascontiguousarray(np.asarray(inputs["L0_attn_Wo"], np.float32))
    bwT = np.ascontiguousarray(np.asarray(inputs["L0_ffn_base_W"], np.float32).T)
    gwT = np.ascontiguousarray(np.asarray(inputs["L0_ffn_gate_W"], np.float32).T)
    aT = np.ascontiguousarray(
        np.asarray(inputs["L0_ffn_lora_A"], np.float32).reshape(ER, D).T)
    bl = np.ascontiguousarray(
        np.asarray(inputs["L0_ffn_lora_B"], np.float32).transpose(0, 2, 1)
        .reshape(ER, D))
    cosk, sink = _rope_tables()
    coskb = np.ascontiguousarray(cosk.astype(ml_dtypes.bfloat16))
    sinkb = np.ascontiguousarray(sink.astype(ml_dtypes.bfloat16))
    ident = np.eye(128, dtype=ml_dtypes.bfloat16)
    ones = np.ones((128, 128), dtype=ml_dtypes.bfloat16)
    E = np.zeros((NEXP, ER), dtype=np.float32)
    for ee in range(NEXP):
        E[ee, ee * RANK:(ee + 1) * RANK] = 2.0
    E = E.astype(ml_dtypes.bfloat16)

    xt_b = [np.ascontiguousarray(X[b].T) for b in range(2)]
    in_maps = []
    rows_all = []
    for c in range(8):
        b, j = c // 4, c % 4
        rows = own_rows(j)
        rows_all.append(rows)
        xtq = np.ascontiguousarray(X[b].T[:, rows])
        in_maps.append(dict(
            xt=xt_b[b], xtq=xtq, wq=wq, wk=wk, wv=wv, wo=wo,
            bwT=bwT, gwT=gwT, aT=aT, bl=bl,
            cosk=coskb, sink=sinkb,
            cosq=np.ascontiguousarray(coskb[:, rows]),
            sinq=np.ascontiguousarray(sinkb[:, rows]),
            masks=_masks_for(j), ident=ident, ones=ones, Emat=E,
        ))
    return in_maps, rows_all


def kernel(**inputs):
    global _last_results
    in_maps, rows_all = _prep_host(inputs)
    nc = build_program()
    res = run_bass_kernel_spmd(nc, in_maps, core_ids=list(range(8)))
    _last_results = res
    out = np.zeros((2, T, D), dtype=np.float32)
    for c in range(8):
        out[c // 4, rows_all[c]] = res.results[c]["out"]
    return out


# version 03
# speedup vs baseline: 1.4722x; 1.4722x over previous
import numpy as np
import ml_dtypes
import concourse.bass as bass
import concourse.mybir as mybir
from concourse.bass_utils import run_bass_kernel_spmd
from contextlib import ExitStack

F32 = mybir.dt.float32
BF16 = mybir.dt.bfloat16
AF = mybir.ActivationFunctionType
ALU = mybir.AluOpType
AX = mybir.AxisListType

T = 4096
D = 4096
OWN = 1024
RS = 0.08838834764831845
EXPB = -14.0
NEXP = 28
RANK = 8
ER = 224

DEBUG = False
STOP_PHASE = 99
_last_results = None

ENGINES = ("tensor", "vector", "scalar", "gpsimd", "sync")


class Sem:
    def __init__(self, h):
        self.h = h
        self.v = 0


class Prog:
    def __init__(self):
        self.streams = {e: [] for e in ENGINES}

    def wait(self, eng, sem, val):
        if val and val > 0:
            self.streams[eng].append(("wait", sem.h, val))

    def op(self, eng, method, *args, sem=None, n=1, **kwargs):
        t = None
        if sem is not None:
            sem.v += n
            t = (sem.h, n)
        self.streams[eng].append(("call", method, args, kwargs, t))
        return sem.v if sem is not None else None


def _replay(E, stream):
    for item in stream:
        if item[0] == "wait":
            E.wait_ge(item[1], item[2])
        else:
            _, method, args, kwargs, t = item
            inst = getattr(E, method)(*args, **kwargs)
            if t is not None:
                inst.then_inc(t[0], t[1])


def own_rows(j):
    gs = np.arange(j, 128, 4)
    return (gs[:, None] * 32 + np.arange(32)[None, :]).reshape(-1)


def _rope_tables():
    inv = 10000.0 ** (-np.arange(64, dtype=np.float64) / 64.0)
    ang = np.arange(T, dtype=np.float64)[:, None] * inv[None, :]
    return (np.cos(ang).T.astype(np.float32),
            np.sin(ang).T.astype(np.float32))


def _masks_for(j):
    p = np.arange(128)
    m = np.zeros((128, 8, 256), dtype=np.float32)
    for r in range(8):
        for u in range(256):
            tp, pq = u // 128, u % 128
            qrel = 512 * tp + 32 * (4 * (pq // 32) + j) + (pq % 32)
            m[:, r, u] = (128 * r + p) <= qrel
    return np.ascontiguousarray(m.reshape(128, 2048)).astype(ml_dtypes.bfloat16)


def build_program():
    nc = bass.Bass("TRN2", target_bir_lowering=False, debug=False)
    scratch_kind = "ExternalOutput" if DEBUG else "Internal"

    d_xt = nc.dram_tensor("xt", [D, T], F32, kind="ExternalInput").ap()
    d_xtq = nc.dram_tensor("xtq", [D, OWN], F32, kind="ExternalInput").ap()
    d_wq = nc.dram_tensor("wq", [D, D], F32, kind="ExternalInput").ap()
    d_wk = nc.dram_tensor("wk", [D, 1024], F32, kind="ExternalInput").ap()
    d_wv = nc.dram_tensor("wv", [D, 1024], F32, kind="ExternalInput").ap()
    d_wo = nc.dram_tensor("wo", [D, D], F32, kind="ExternalInput").ap()
    d_bwT = nc.dram_tensor("bwT", [D, D], F32, kind="ExternalInput").ap()
    d_gwT = nc.dram_tensor("gwT", [D, NEXP], F32, kind="ExternalInput").ap()
    d_aT = nc.dram_tensor("aT", [D, ER], F32, kind="ExternalInput").ap()
    d_bl = nc.dram_tensor("bl", [ER, D], F32, kind="ExternalInput").ap()
    d_cosk = nc.dram_tensor("cosk", [64, T], BF16, kind="ExternalInput").ap()
    d_sink = nc.dram_tensor("sink", [64, T], BF16, kind="ExternalInput").ap()
    d_cosq = nc.dram_tensor("cosq", [64, OWN], BF16, kind="ExternalInput").ap()
    d_sinq = nc.dram_tensor("sinq", [64, OWN], BF16, kind="ExternalInput").ap()
    d_masks = nc.dram_tensor("masks", [128, 2048], BF16, kind="ExternalInput").ap()
    d_ident = nc.dram_tensor("ident", [128, 128], BF16, kind="ExternalInput").ap()
    d_ones = nc.dram_tensor("ones", [128, 128], BF16, kind="ExternalInput").ap()
    d_E = nc.dram_tensor("Emat", [NEXP, ER], BF16, kind="ExternalInput").ap()
    d_out = nc.dram_tensor("out", [OWN, D], F32, kind="ExternalOutput").ap()

    s_xtbf = nc.dram_tensor("s_xtbf", [D, T], BF16, kind=scratch_kind).ap()
    s_kt = nc.dram_tensor("s_kt", [1024, T], BF16, kind=scratch_kind).ap()
    s_v = nc.dram_tensor("s_v", [T, 1024], BF16, kind=scratch_kind).ap()
    s_qt = nc.dram_tensor("s_qt", [D, OWN], BF16, kind=scratch_kind).ap()
    s_hpre = nc.dram_tensor("s_hpre", [D, OWN], BF16, kind=scratch_kind).ap()
    s_h2pre = nc.dram_tensor("s_h2pre", [D, OWN], BF16, kind=scratch_kind).ap()
    if DEBUG:
        s_h1 = nc.dram_tensor("s_h1", [D, OWN], BF16, kind="ExternalOutput").ap()
        s_dbg = nc.dram_tensor("s_dbg", [128, 4096], BF16,
                               kind="ExternalOutput").ap()
        s_dbg2 = nc.dram_tensor("s_dbg2", [128, 256], F32,
                                kind="ExternalOutput").ap()
        s_dbg3 = nc.dram_tensor("s_dbg3", [128, 2048], BF16,
                                kind="ExternalOutput").ap()
        s_h1b = nc.dram_tensor("s_h1b", [D, OWN], BF16,
                               kind="ExternalOutput").ap()

    P = Prog()

    with ExitStack() as st:
        e = st.enter_context

        AR1 = e(nc.sbuf_tensor("AR1", [128, 32768], BF16))
        AR2 = e(nc.sbuf_tensor("AR2", [128, 32768], BF16))
        CSK = e(nc.sbuf_tensor("CSK", [64, 8192], BF16))
        CSQ = e(nc.sbuf_tensor("CSQ", [64, 2048], BF16))
        MSK = e(nc.sbuf_tensor("MSK", [128, 2048], BF16))
        HPB = e(nc.sbuf_tensor("HPB", [128, 4096], BF16))
        KVB = e(nc.sbuf_tensor("KVB", [128, 4096], BF16))
        SQB = e(nc.sbuf_tensor("SQB", [128, 1024], BF16))
        LTM = e(nc.sbuf_tensor("LTM", [128, 1024], BF16))
        IDN = e(nc.sbuf_tensor("IDN", [128, 128], BF16))
        ONE = e(nc.sbuf_tensor("ONE", [128, 128], BF16))
        EMB = e(nc.sbuf_tensor("EMB", [NEXP, ER], BF16))
        ONB = e(nc.sbuf_tensor("ONB", [128, 2048], F32))
        XRB = e(nc.sbuf_tensor("XRB", [128, 1024], F32))
        LGB = e(nc.sbuf_tensor("LGB", [128, 128], F32))
        RCB = e(nc.sbuf_tensor("RCB", [128, 512], F32))
        GTM = e(nc.sbuf_tensor("GTM", [128, 16], F32))
        SRW = e(nc.sbuf_tensor("SRW", [1, 3072], F32))
        ONF = e(nc.sbuf_tensor("ONF", [1, 128], F32))
        BIA = e(nc.sbuf_tensor("BIA", [128, 2], F32))

        sG = Sem(e(nc.semaphore("sG")))
        sS = Sem(e(nc.semaphore("sS")))
        sPE = Sem(e(nc.semaphore("sPE")))
        sACT = Sem(e(nc.semaphore("sACT")))
        sDVE = Sem(e(nc.semaphore("sDVE")))

        v_c = 0
        for dst, src in [(CSK[:, 0:T], d_cosk), (CSK[:, T:2 * T], d_sink),
                         (CSQ[:, 0:OWN], d_cosq), (CSQ[:, OWN:2 * OWN], d_sinq),
                         (MSK[:, :], d_masks), (IDN[:, :], d_ident),
                         (ONE[:, :], d_ones), (EMB[:, :], d_E)]:
            v_c = P.op("sync", "dma_start", dst, src, sem=sS, n=16)
        P.op("vector", "memset", BIA[:, 0:1], EXPB, sem=sDVE)
        P.op("vector", "memset", BIA[:, 1:2], 0.0, sem=sDVE)
        v_bia = P.op("vector", "memset", ONF[:, :], 1.0, sem=sDVE)
        P.wait("tensor", sS, v_c)
        P.wait("vector", sS, v_c)
        P.wait("scalar", sS, v_c)

        def ar1(c0, c1):
            return AR1[:, c0:c1]

        def ar2(c0, c1):
            return AR2[:, c0:c1]

        def dvs(method, *args, **kwargs):
            v = P.op("vector", method, *args, sem=sDVE, **kwargs)
            P.wait("vector", sDVE, v)
            return v

        v_pe_k = [[0] * 8 for _ in range(8)]
        v_rope_k = [[0] * 8 for _ in range(8)]
        if STOP_PHASE >= 1:
            with nc.psum_tensor("ps1A", [128, 512], F32) as psA, \
                 nc.psum_tensor("ps1B", [128, 512], F32) as psB:
                psK = [psA, psB]
                v_wk1 = P.op("gpsimd", "dma_start",
                             AR2[:, :].rearrange("p (dt o) -> p dt o", dt=32)
                             [:, :, 0:512],
                             d_wk.rearrange("(dt p) o -> p dt o", p=128)
                             [:, :, 0:512], sem=sG, n=16)
                v_wk2 = P.op("gpsimd", "dma_start",
                             AR2[:, :].rearrange("p (dt o) -> p dt o", dt=32)
                             [:, :, 512:1024],
                             d_wk.rearrange("(dt p) o -> p dt o", p=128)
                             [:, :, 512:1024], sem=sG, n=16)
                v_xt = [0] * 8
                v_sxtbf = [0] * 8
                v_skt = [[0, 0] for _ in range(8)]
                for tc in range(8):
                    sl = tc % 2
                    x0 = sl * 16384
                    if tc >= 2:
                        P.wait("gpsimd", sPE, v_pe_k[tc - 2][7])
                        P.wait("gpsimd", sS, v_sxtbf[tc - 2])
                    v_xt[tc] = P.op(
                        "gpsimd", "dma_start",
                        ar1(x0, x0 + 16384).rearrange("p (dt t) -> p dt t", dt=32),
                        d_xt.rearrange("(dt p) t -> p dt t", p=128)
                        [:, :, tc * 512:(tc + 1) * 512], sem=sG, n=16)
                    P.wait("sync", sG, v_xt[tc])
                    v_sxtbf[tc] = P.op(
                        "sync", "dma_start",
                        s_xtbf.rearrange("(dt p) t -> p dt t", p=128)
                        [:, :, tc * 512:(tc + 1) * 512],
                        ar1(x0, x0 + 16384).rearrange("p (dt t) -> p dt t", dt=32),
                        sem=sS, n=16)
                    P.wait("tensor", sG, max(v_xt[tc], v_wk1))
                    for h in range(8):
                        if h == 4 and tc == 0:
                            P.wait("tensor", sG, v_wk2)
                        gh = tc * 8 + h
                        if gh >= 2:
                            h2t, h2h = divmod(gh - 2, 8)
                            P.wait("tensor", sDVE, v_rope_k[h2t][h2h])
                        for dt in range(32):
                            sm = sPE if dt == 31 else None
                            v = P.op("tensor", "matmul", psK[h % 2][:, :],
                                     ar2(dt * 1024 + h * 128,
                                         dt * 1024 + (h + 1) * 128),
                                     ar1(x0 + dt * 512, x0 + (dt + 1) * 512),
                                     start=(dt == 0), stop=(dt == 31), sem=sm)
                            if sm:
                                v_pe_k[tc][h] = v
                        hb, half = divmod(h, 4)
                        kb = hb * 2048
                        P.wait("vector", sPE, v_pe_k[tc][h])
                        if half == 0:
                            prev = (tc * 2 + hb) - 2
                            if prev >= 0:
                                P.wait("vector", sS, v_skt[prev // 2][prev % 2])
                        ps = psK[h % 2]
                        cosb = CSK[:, tc * 512:(tc + 1) * 512]
                        sinb = CSK[:, T + tc * 512:T + (tc + 1) * 512]
                        t1 = ONB[0:64, 0:512]
                        t2 = ONB[0:64, 512:1024]
                        dvs("tensor_tensor", t1, ps[0:64, :], cosb, ALU.mult)
                        dvs("tensor_tensor", t2, ps[64:128, :], sinb, ALU.mult)
                        dvs("tensor_tensor",
                            KVB[0:64, kb + half * 512:kb + (half + 1) * 512],
                            t1, t2, ALU.subtract)
                        dvs("tensor_tensor", t1, ps[0:64, :], sinb, ALU.mult)
                        dvs("tensor_tensor", t2, ps[64:128, :], cosb, ALU.mult)
                        v_rope_k[tc][h] = dvs(
                            "tensor_tensor",
                            KVB[64:128, kb + half * 512:kb + (half + 1) * 512],
                            t1, t2, ALU.add)
                        if half == 3:
                            P.wait("sync", sDVE, v_rope_k[tc][h])
                            v_skt[tc][hb] = P.op(
                                "sync", "dma_start",
                                s_kt.rearrange("(h p) t -> p h t", p=128)
                                [:, hb * 4:hb * 4 + 4, tc * 512:(tc + 1) * 512],
                                KVB[:, kb:kb + 2048]
                                .rearrange("p (h t) -> p h t", h=4),
                                sem=sS, n=16)

        v_pe_v = [[0] * 4 for _ in range(8)]
        v_act_v = [[0] * 4 for _ in range(8)]
        v_sv = [[0] * 4 for _ in range(8)]
        if STOP_PHASE >= 2:
            with nc.psum_tensor("ps2A", [128, 1024], F32) as psA, \
                 nc.psum_tensor("ps2B", [128, 1024], F32) as psB:
                psV = [psA, psB]
                P.wait("gpsimd", sPE, v_pe_k[7][7])
                v_wv1 = P.op("gpsimd", "dma_start",
                             AR2[:, :].rearrange("p (dt o) -> p dt o", dt=32)
                             [:, :, 0:512],
                             d_wv.rearrange("(dt p) o -> p dt o", p=128)
                             [:, :, 0:512], sem=sG, n=16)
                v_wv = P.op("gpsimd", "dma_start",
                            AR2[:, :].rearrange("p (dt o) -> p dt o", dt=32)
                            [:, :, 512:1024],
                            d_wv.rearrange("(dt p) o -> p dt o", p=128)
                            [:, :, 512:1024], sem=sG, n=16)
                v_xbf = [0] * 8
                for tc in range(8):
                    sl = tc % 2
                    x0 = sl * 16384
                    if tc >= 2:
                        P.wait("sync", sPE, v_pe_v[tc - 2][3])
                    else:
                        P.wait("sync", sPE, v_pe_k[7][7])
                    P.wait("sync", sS, v_sxtbf[tc])
                    v_xbf[tc] = P.op(
                        "sync", "dma_start",
                        ar1(x0, x0 + 16384).rearrange("p (dt t) -> p dt t", dt=32),
                        s_xtbf.rearrange("(dt p) t -> p dt t", p=128)
                        [:, :, tc * 512:(tc + 1) * 512], sem=sS, n=16)
                    P.wait("tensor", sS, v_xbf[tc])
                    P.wait("tensor", sG, v_wv1 if tc == 0 else v_wv)
                    for tt in range(4):
                        gv = tc * 4 + tt
                        if gv >= 2:
                            p2t, p2h = divmod(gv - 2, 4)
                            P.wait("tensor", sACT, v_act_v[p2t][p2h])
                        for oc in range(2):
                            if tc == 0 and tt == 0 and oc == 1:
                                P.wait("tensor", sG, v_wv)
                            for dt in range(32):
                                sm = sPE if (dt == 31 and oc == 1) else None
                                v = P.op(
                                    "tensor", "matmul",
                                    psV[gv % 2][:, oc * 512:(oc + 1) * 512],
                                    ar1(x0 + dt * 512 + tt * 128,
                                        x0 + dt * 512 + (tt + 1) * 128),
                                    ar2(dt * 1024 + oc * 512,
                                        dt * 1024 + (oc + 1) * 512),
                                    start=(dt == 0), stop=(dt == 31), sem=sm)
                                if sm:
                                    v_pe_v[tc][tt] = v
                        P.wait("scalar", sPE, v_pe_v[tc][tt])
                        if gv >= 2:
                            p2t, p2h = divmod(gv - 2, 4)
                            P.wait("scalar", sS, v_sv[p2t][p2h])
                        v_act_v[tc][tt] = P.op(
                            "scalar", "copy",
                            KVB[:, (gv % 2) * 1024:(gv % 2 + 1) * 1024],
                            psV[gv % 2][:, :], sem=sACT)
                        P.wait("sync", sACT, v_act_v[tc][tt])
                        r0 = tc * 512 + tt * 128
                        v_sv[tc][tt] = P.op(
                            "sync", "dma_start", s_v[r0:r0 + 128, :],
                            KVB[:, (gv % 2) * 1024:(gv % 2 + 1) * 1024],
                            sem=sS, n=16)

        v_pe_q = [0] * 32
        v_rope_q = [0] * 32
        v_sqt = [0] * 8
        if STOP_PHASE >= 3:
            with nc.psum_tensor("ps3A", [128, 1024], F32) as psA, \
                 nc.psum_tensor("ps3B", [128, 1024], F32) as psB:
                psQ = [psA, psB]
                P.wait("gpsimd", sPE, v_pe_v[7][3])

                def _xtq_dma(hf):
                    return P.op(
                        "gpsimd", "dma_start",
                        AR1[:, :].rearrange("p (dt t) -> p dt t", dt=32)
                        [:, :, hf * 512:(hf + 1) * 512],
                        d_xtq.rearrange("(dt p) t -> p dt t", p=128)
                        [:, :, hf * 512:(hf + 1) * 512], sem=sG, n=16)

                v_xtqh = [_xtq_dma(0), 0]
                v_wq = [0] * 32
                for ot in range(32):
                    sl = ot % 2
                    w0 = sl * 4096
                    if ot >= 2:
                        P.wait("gpsimd", sPE, v_pe_q[ot - 2])
                    else:
                        P.wait("gpsimd", sACT, v_act_v[7][3])
                    v_wq[ot] = P.op(
                        "gpsimd", "dma_start",
                        ar2(w0, w0 + 4096).rearrange("p (dt o) -> p dt o", dt=32),
                        d_wq.rearrange("(dt p) o -> p dt o", p=128)
                        [:, :, ot * 128:(ot + 1) * 128], sem=sG, n=16)
                    if ot == 0:
                        v_xtqh[1] = _xtq_dma(1)
                    P.wait("tensor", sG, max(v_wq[ot], v_xtqh[0]))
                    if ot >= 2:
                        P.wait("tensor", sDVE, v_rope_q[ot - 2])
                    for half in range(2):
                        if ot == 0 and half == 1:
                            P.wait("tensor", sG, v_xtqh[1])
                        for dt in range(32):
                            sm = sPE if (dt == 31 and half == 1) else None
                            v = P.op("tensor", "matmul",
                                     psQ[sl][:, half * 512:(half + 1) * 512],
                                     ar2(w0 + dt * 128, w0 + (dt + 1) * 128),
                                     ar1(dt * 1024 + half * 512,
                                         dt * 1024 + (half + 1) * 512),
                                     start=(dt == 0), stop=(dt == 31), sem=sm)
                            if sm:
                                v_pe_q[ot] = v
                    P.wait("vector", sPE, v_pe_q[ot])
                    if ot >= 4:
                        P.wait("vector", sS, v_sqt[(ot - 4) // 4])
                    ps = psQ[sl]
                    cosb = CSQ[:, 0:OWN]
                    sinb = CSQ[:, OWN:2 * OWN]
                    q0 = (ot % 4) * 1024
                    t1 = ONB[0:64, 0:1024]
                    t2 = ONB[0:64, 1024:2048]
                    dvs("tensor_tensor", t1, ps[0:64, :], cosb, ALU.mult)
                    dvs("tensor_tensor", t2, ps[64:128, :], sinb, ALU.mult)
                    dvs("tensor_tensor", HPB[0:64, q0:q0 + 1024],
                        t1, t2, ALU.subtract)
                    dvs("tensor_tensor", t1, ps[0:64, :], sinb, ALU.mult)
                    dvs("tensor_tensor", t2, ps[64:128, :], cosb, ALU.mult)
                    v_rope_q[ot] = dvs("tensor_tensor",
                                       HPB[64:128, q0:q0 + 1024], t1, t2,
                                       ALU.add)
                    if ot % 4 == 3:
                        P.wait("sync", sDVE, v_rope_q[ot])
                        v_sqt[ot // 4] = P.op(
                            "sync", "dma_start",
                            s_qt.rearrange("(q p) t -> p q t", p=128)
                            [:, ot - 3:ot + 1, :],
                            HPB[:, :].rearrange("p (q t) -> p q t", q=4),
                            sem=sS, n=16)

        v_onorm = {}
        v_head_pe = [0] * 8
        if STOP_PHASE >= 4:
            with nc.psum_tensor("ps4A", [128, 1024], F32) as psA, \
                 nc.psum_tensor("ps4B", [128, 1024], F32) as psB, \
                 nc.psum_tensor("ps4C", [128, 1024], F32) as psC, \
                 nc.psum_tensor("ps4D1", [128, 512], F32) as psD1, \
                 nc.psum_tensor("ps4D2", [128, 512], F32) as psD2:
                psS = [psA, psB, psC]
                psDA = [psD1, psD2]
                v_head_ld = [0] * 8
                gi = 0
                v_sgrp = {}
                v_exp = {}
                v_mask = {}
                v_av = {}
                for h in range(8):
                    sl = h % 2
                    kt0 = sl * 4096
                    vt0 = 8192 + sl * 4096
                    qt0 = 16384 + sl * 4096
                    if h >= 2:
                        P.wait("sync", sPE, v_head_pe[h - 2])
                    elif h == 0:
                        P.wait("sync", sDVE, v_rope_q[31])
                        P.wait("sync", sS, v_skt[7][1])
                        P.wait("sync", sS, v_sv[7][3])
                    P.wait("sync", sS, v_sqt[h])
                    P.op("sync", "dma_start", ar1(kt0, kt0 + 4096),
                         s_kt[h * 128:(h + 1) * 128, :], sem=sS, n=16)
                    P.op("sync", "dma_start",
                         ar1(vt0, vt0 + 4096).rearrange("p (tt o) -> p tt o", tt=32),
                         s_v.rearrange("(tt p) o -> p tt o", p=128)
                         [:, :, h * 128:(h + 1) * 128], sem=sS, n=16)
                    v_head_ld[h] = P.op(
                        "sync", "dma_start",
                        ar1(qt0, qt0 + 4096).rearrange("p (g t) -> p g t", g=4),
                        s_qt.rearrange("(q p) t -> p q t", p=128)
                        [:, 4 * h:4 * h + 4, :], sem=sS, n=16)
                    P.wait("tensor", sS, v_head_ld[h])

                    def emit_S(gi, kt0, qt0, g, c, grp, ntk):
                        rhs = ar1(qt0 + g * 1024 + c * 256,
                                  qt0 + g * 1024 + (c + 1) * 256)
                        if gi >= 3:
                            P.wait("tensor", sACT, v_exp[gi - 3])
                        for i in range(4):
                            k = 4 * grp + i
                            sm = sPE if i == 3 else None
                            v = P.op("tensor", "matmul",
                                     psS[gi % 3][:, i * 256:(i + 1) * 256],
                                     ar1(kt0 + k * 128, kt0 + (k + 1) * 128),
                                     rhs, start=(i % 2 == 0), stop=True,
                                     skip_group_check=True, sem=sm)
                            if sm:
                                v_sgrp[gi] = v
                        if gi == 0:
                            P.wait("scalar", sS, v_sv[7][3])
                        P.wait("scalar", sPE, v_sgrp[gi])
                        if gi >= 4:
                            P.wait("scalar", sPE, v_av[gi - 4])
                        p0 = (gi % 4) * 1024
                        v_exp[gi] = P.op("scalar", "activation",
                                         KVB[:, p0:p0 + 1024],
                                         psS[gi % 3][:, :], AF.Exp,
                                         bias=BIA[:, 0:1], scale=RS, sem=sACT)
                        if grp >= (ntk // 4) - 2:
                            r0 = 4 * grp - (ntk - 8)
                            P.wait("vector", sACT, v_exp[gi])
                            v_mask[gi] = P.op(
                                "vector", "tensor_tensor",
                                KVB[:, p0:p0 + 1024], KVB[:, p0:p0 + 1024],
                                MSK[:, r0 * 256:r0 * 256 + 1024],
                                ALU.mult, sem=sDVE)

                    def emit_DAV(gi, vt0, hq, c, ci, grp, ntk):
                        p0 = (gi % 4) * 1024
                        if grp >= (ntk // 4) - 2:
                            P.wait("tensor", sDVE, v_mask[gi])
                        else:
                            P.wait("tensor", sACT, v_exp[gi])
                        if grp == 0 and ci >= 2:
                            P.wait("tensor", sDVE, v_onorm[ci - 2])
                        for i in range(4):
                            k = 4 * grp + i
                            P.op("tensor", "matmul",
                                 psDA[c % 2][:, 0:256], ONE[:, :],
                                 KVB[:, p0 + i * 256:p0 + (i + 1) * 256],
                                 start=(k == 0), stop=(k == ntk - 1),
                                 skip_group_check=True)
                        for i in range(4):
                            k = 4 * grp + i
                            sm = sPE if i == 3 else None
                            v = P.op("tensor", "matmul",
                                     psDA[c % 2][:, 256:512],
                                     ar1(vt0 + k * 128, vt0 + (k + 1) * 128),
                                     KVB[:, p0 + i * 256:p0 + (i + 1) * 256],
                                     start=False, stop=(k == ntk - 1),
                                     skip_group_check=True, sem=sm)
                            if sm:
                                v_av[gi] = v
                        if grp == (ntk // 4) - 1:
                            P.wait("vector", sPE, v_av[gi])
                            if ci == 0:
                                P.wait("vector", sPE, v_pe_q[31])
                            rc = RCB[:, (c % 2) * 256:(c % 2 + 1) * 256]
                            dvs("reciprocal", rc, psDA[c % 2][:, 0:256])
                            v_onorm[ci] = dvs(
                                "tensor_tensor",
                                ar2(hq * 1024 + c * 256,
                                    hq * 1024 + (c + 1) * 256),
                                psDA[c % 2][:, 256:512], rc, ALU.mult)

                    pend = None
                    for g in range(4):
                        for c in range(4):
                            ntk = 8 * c + 8
                            ci = (h * 4 + g) * 4 + c
                            hq = 4 * h + g
                            for grp in range(ntk // 4):
                                emit_S(gi, kt0, qt0, g, c, grp, ntk)
                                if pend is not None:
                                    emit_DAV(*pend)
                                pend = (gi, vt0, hq, c, ci, grp, ntk)
                                gi += 1
                    emit_DAV(*pend)
                    v_head_pe[h] = v_av[gi - 1]

        v_pe_wo = [0] * 32
        v_pe_st = [0] * 32
        v_mrb = 0
        if STOP_PHASE >= 5:
            st56 = ExitStack()
            psA = st56.enter_context(nc.psum_tensor("ps5A", [128, 1024], F32))
            psB = st56.enter_context(nc.psum_tensor("ps5B", [128, 1024], F32))
            psSx = st56.enter_context(nc.psum_tensor("ps5Sx", [128, 1024], F32))
            psSq = st56.enter_context(nc.psum_tensor("ps5Sq", [128, 1024], F32))
            if True:
                psW = [psA, psB]
                v_wo = [0] * 32
                v_xr = [0] * 32
                v_hp = [0] * 32
                v_shp = [0] * 8
                for ot in range(32):
                    sl = ot % 2
                    w0 = sl * 4096
                    if ot >= 2:
                        P.wait("gpsimd", sPE, v_pe_wo[ot - 2])
                    else:
                        P.wait("gpsimd", sPE, v_head_pe[7])
                    v_wo[ot] = P.op(
                        "gpsimd", "dma_start",
                        ar1(w0, w0 + 4096).rearrange("p (dt o) -> p dt o", dt=32),
                        d_wo.rearrange("(dt p) o -> p dt o", p=128)
                        [:, :, ot * 128:(ot + 1) * 128], sem=sG, n=16)
                    if ot >= 1:
                        P.wait("sync", sDVE, v_hp[ot - 1])
                    v_xr[ot] = P.op("sync", "dma_start", XRB[:, 0:1024],
                                    d_xtq[ot * 128:(ot + 1) * 128, :],
                                    sem=sS, n=16)
                    P.wait("tensor", sG, v_wo[ot])
                    if ot == 0:
                        P.wait("tensor", sDVE, v_onorm[127])
                    if ot >= 2:
                        P.wait("tensor", sDVE, v_hp[ot - 2])
                    for half in range(2):
                        for dt in range(32):
                            sm = sPE if (dt == 31 and half == 1) else None
                            v = P.op("tensor", "matmul",
                                     psW[sl][:, half * 512:(half + 1) * 512],
                                     ar1(w0 + dt * 128, w0 + (dt + 1) * 128),
                                     ar2(dt * 1024 + half * 512,
                                         dt * 1024 + (half + 1) * 512),
                                     start=(dt == 0), stop=(dt == 31), sem=sm)
                            if sm:
                                v_pe_wo[ot] = v
                    P.wait("vector", sPE, v_pe_wo[ot])
                    P.wait("vector", sS, v_xr[ot])
                    if ot == 0:
                        P.wait("vector", sS, v_sqt[7])
                    if ot >= 1:
                        P.wait("vector", sPE, v_pe_st[ot - 1])
                    if ot >= 4:
                        P.wait("vector", sS, v_shp[(ot - 4) // 4])
                    hp = HPB[:, (ot % 4) * 1024:(ot % 4 + 1) * 1024]
                    dvs("tensor_tensor", hp, psW[sl][:, :],
                        XRB[:, 0:1024], ALU.add)
                    v_hp[ot] = dvs("tensor_tensor", SQB[:, :], hp, hp, ALU.mult)
                    P.wait("tensor", sDVE, v_hp[ot])
                    for half in range(2):
                        P.op("tensor", "matmul",
                             psSx[0:1, half * 512:(half + 1) * 512],
                             ONE[:, 0:1], hp[:, half * 512:(half + 1) * 512],
                             start=(ot == 0), stop=(ot == 31),
                             skip_group_check=True)
                        sm = sPE if half == 1 else None
                        v = P.op("tensor", "matmul",
                                 psSq[0:1, half * 512:(half + 1) * 512],
                                 ONE[:, 0:1], SQB[:, half * 512:(half + 1) * 512],
                                 start=(ot == 0), stop=(ot == 31),
                                 skip_group_check=True, sem=sm)
                        if sm:
                            v_pe_st[ot] = v
                    if ot % 4 == 3:
                        P.wait("sync", sDVE, v_hp[ot])
                        v_shp[ot // 4] = P.op(
                            "sync", "dma_start",
                            s_hpre.rearrange("(o p) t -> p o t", p=128)
                            [:, ot - 3:ot + 1, :],
                            HPB[:, :].rearrange("p (o t) -> p o t", o=4),
                            sem=sS, n=16)
                v_mrb = _ln_finalize(P, psSx, psSq, v_pe_st[31], SRW,
                                     BIA, ONF, sPE, sACT, sDVE, v_bia)

        GW0 = 0
        AT0 = 1024
        BL0 = 8192
        GE0 = 16384
        HS0 = 18432
        GN0 = 20480
        GT0 = 22528
        WS0 = 24576

        v_h1 = [0] * 32
        v_ffnw = 0
        if STOP_PHASE >= 6:
            P.wait("gpsimd", sPE, v_pe_wo[31])
            P.op("gpsimd", "dma_start",
                 ar2(GW0, GW0 + 896).rearrange("p (dt e) -> p dt e", dt=32),
                 d_gwT.rearrange("(dt p) e -> p dt e", p=128), sem=sG, n=16)
            P.op("gpsimd", "dma_start",
                 ar2(AT0, AT0 + 7168).rearrange("p (dt r) -> p dt r", dt=32),
                 d_aT.rearrange("(dt p) r -> p dt r", p=128), sem=sG, n=16)
            P.op("gpsimd", "dma_start", ar2(BL0, BL0 + 4096),
                 d_bl[0:128, :], sem=sG, n=16)
            v_ffnw = P.op("gpsimd", "dma_start",
                          AR2[0:96, BL0 + 4096:BL0 + 8192],
                          d_bl[128:224, :], sem=sG, n=16)
            v_hp6 = [0] * 32
            for i in range(34):
                if i < 32:
                    if i < 4:
                        P.wait("sync", sPE, v_pe_st[31])
                    else:
                        P.wait("sync", sDVE, v_h1[i - 4])
                    P.wait("sync", sS, v_shp[i // 4])
                    v_hp6[i] = P.op("sync", "dma_start",
                                    HPB[:, (i % 4) * 1024:(i % 4 + 1) * 1024],
                                    s_hpre[i * 128:(i + 1) * 128, :],
                                    sem=sS, n=16)
                if i >= 2:
                    dt = i - 2
                    sl = dt % 4
                    P.wait("vector", sS, v_hp6[min(dt + 2, 31)])
                    if dt == 0:
                        P.wait("vector", sPE, v_pe_wo[31])
                        P.wait("vector", sPE, v_mrb)
                    hp = HPB[:, sl * 1024:(sl + 1) * 1024]
                    dvs("tensor_tensor", LTM[:, :], hp, psSx[:, 0:1024],
                        ALU.subtract)
                    v_h1[dt] = dvs("tensor_tensor",
                                   ar1(dt * 1024, (dt + 1) * 1024),
                                   LTM[:, :], psSq[:, 0:1024], ALU.mult)
            if DEBUG:
                P.wait("sync", sDVE, v_h1[31])
                for dt in range(32):
                    vrd = P.op("sync", "dma_start", HPB[:, 0:1024],
                               s_hpre[dt * 128:(dt + 1) * 128, :],
                               sem=sS, n=16)
                    P.wait("vector", sS, vrd)
                    dvs("tensor_tensor", LTM[:, :], HPB[:, 0:1024],
                        psSx[:, 0:1024], ALU.subtract)
                    v_dbg = dvs("tensor_tensor", SQB[:, :],
                                LTM[:, :], psSq[:, 0:1024], ALU.mult)
                    P.wait("sync", sDVE, v_dbg)
                    vwr = P.op("sync", "dma_start",
                               s_h1b[dt * 128:(dt + 1) * 128, :], SQB[:, :],
                               sem=sS, n=16)
                    P.wait("vector", sS, vwr)
                v1 = dvs("tensor_tensor", LTM[:, :], psSx[:, 0:1024],
                         LTM[:, :], ALU.bypass)
                P.wait("sync", sDVE, v1)
                vda = P.op("sync", "dma_start", s_dbg3[:, 0:1024], LTM[:, :],
                           sem=sS, n=16)
                P.wait("vector", sS, vda)
                v2 = dvs("tensor_tensor", LTM[:, :], psSq[:, 0:1024],
                         LTM[:, :], ALU.bypass)
                P.wait("sync", sDVE, v2)
                P.op("sync", "dma_start", s_dbg3[:, 1024:2048], LTM[:, :],
                     sem=sS, n=16)
            st56.close()

        if DEBUG and STOP_PHASE >= 6:
            P.wait("sync", sDVE, v_h1[31])
            P.op("sync", "dma_start",
                 s_h1.rearrange("(dt p) t -> p dt t", p=128),
                 AR1[:, :].rearrange("p (dt t) -> p dt t", dt=32),
                 sem=sS, n=16)

        v_hsc = [0, 0]
        v_pe_lora = 0
        if STOP_PHASE >= 7:
            with nc.psum_tensor("ps7L", [128, 1024], F32) as psL, \
                 nc.psum_tensor("ps7T", [128, 2048], BF16) as psT, \
                 nc.psum_tensor("ps7E1", [128, 1024], F32) as psE1, \
                 nc.psum_tensor("ps7E2", [128, 1024], F32) as psE2:
                v_lg = [0] * 8
                v_g = [0] * 8
                v_gt = [0] * 8
                v_gte = [0] * 8
                P.wait("tensor", sDVE, v_h1[31])
                P.wait("tensor", sG, v_ffnw)
                for tt in range(8):
                    lb = (tt % 2) * 512
                    if tt >= 2:
                        P.wait("tensor", sDVE, v_g[tt - 2])
                    for dt in range(32):
                        sm = sPE if dt == 31 else None
                        v = P.op("tensor", "matmul", psL[:, lb:lb + NEXP],
                                 ar1(dt * 1024 + tt * 128,
                                     dt * 1024 + (tt + 1) * 128),
                                 ar2(GW0 + dt * 28, GW0 + (dt + 1) * 28),
                                 start=(dt == 0), stop=(dt == 31), sem=sm)
                        if sm:
                            v_lg[tt] = v
                    P.wait("vector", sPE, v_lg[tt])

                    def dss(method, *args, **kwargs):
                        v = P.op("vector", method, *args, sem=sDVE, **kwargs)
                        P.wait("vector", sDVE, v)
                        return v

                    lg = LGB[:, 0:NEXP]
                    m1 = GTM[:, 0:1]
                    m1n = GTM[:, 1:2]
                    cnt = GTM[:, 2:3]
                    m2 = GTM[:, 3:4]
                    is2 = GTM[:, 4:5]
                    thr = GTM[:, 5:6]
                    zs = GTM[:, 6:7]
                    rz = GTM[:, 7:8]
                    ge1 = LGB[:, 32:60]
                    wv_ = LGB[:, 64:92]
                    kp = LGB[:, 96:124]
                    dss("tensor_tensor", lg, psL[:, lb:lb + NEXP], lg,
                        ALU.bypass)
                    dss("tensor_reduce", m1, lg, AX.X, ALU.max)
                    v_pre = dss("tensor_scalar", m1n, m1, -1.0, None, ALU.mult)
                    dss("tensor_scalar", ge1, lg, m1, None,
                        ALU.is_ge, op1=ALU.add, accum_out=cnt)
                    P.wait("scalar", sDVE, v_pre)
                    v_w = P.op("scalar", "activation", wv_, lg, AF.Exp,
                               bias=m1n, scale=1.0, sem=sACT)
                    dss("scalar_tensor_tensor", kp, ge1, -1e30, lg,
                        ALU.mult, ALU.add)
                    dss("tensor_reduce", m2, kp, AX.X, ALU.max)
                    dss("tensor_scalar", is2, cnt, 1.5, None, ALU.is_ge)
                    dss("tensor_tensor", thr, m1, m2, ALU.subtract)
                    dss("tensor_tensor", thr, is2, thr, ALU.mult)
                    dss("tensor_tensor", thr, thr, m2, ALU.add)
                    dss("tensor_scalar", kp, psL[:, lb:lb + NEXP],
                        thr, None, ALU.is_ge)
                    P.wait("vector", sACT, v_w)
                    dss("tensor_tensor", wv_, wv_, kp, ALU.mult)
                    dss("tensor_reduce", zs, wv_, AX.X, ALU.add)
                    dss("reciprocal", rz, zs)
                    v_g[tt] = dss("tensor_scalar",
                                  ar2(GN0 + tt * 28, GN0 + (tt + 1) * 28),
                                  wv_, rz, None, ALU.mult)
                    if DEBUG:
                        P.wait("sync", sDVE, v_g[tt])
                        vd = P.op("sync", "dma_start",
                                  s_dbg2[:, tt * 16:(tt + 1) * 16],
                                  GTM[:, 0:16], sem=sS, n=16)
                        if tt == 0:
                            vd = P.op("sync", "dma_start", s_dbg2[:, 128:256],
                                      LGB[:, :], sem=sS, n=16)
                        P.wait("vector", sS, vd)
                    P.wait("tensor", sDVE, v_g[tt])
                    if tt >= 2:
                        P.wait("tensor", sDVE, v_gte[tt - 2])
                    pT = psT[0:28, (tt % 2) * 1024:(tt % 2) * 1024 + 128]
                    v_gt[tt] = P.op("tensor", "transpose", pT,
                                    ar2(GN0 + tt * 28, GN0 + (tt + 1) * 28),
                                    IDN[:, :], sem=sPE)
                    P.wait("vector", sPE, v_gt[tt])
                    gslc = AR2[0:28, GT0 + tt * 128:GT0 + (tt + 1) * 128]
                    v_gte[tt] = dss("tensor_tensor", gslc, pT[0:28, :], gslc,
                                    ALU.bypass)
                P.wait("tensor", sDVE, v_gte[7])
                psEx = [psE1, psE2]
                v_ex = [0, 0]
                for ert in range(2):
                    m = 128 if ert == 0 else 96
                    for half in range(2):
                        sm = sPE if half == 1 else None
                        v = P.op("tensor", "matmul",
                                 psEx[ert][0:m, half * 512:(half + 1) * 512],
                                 EMB[:, ert * 128:ert * 128 + m],
                                 AR2[0:28, GT0 + half * 512:GT0 + (half + 1) * 512],
                                 start=True, stop=True, sem=sm)
                        if sm:
                            v_ex[ert] = v
                v_gee = [0, 0]
                for ert in range(2):
                    m = 128 if ert == 0 else 96
                    P.wait("vector", sPE, v_ex[ert])
                    gsl = AR2[0:m, GE0 + ert * 1024:GE0 + (ert + 1) * 1024]
                    v_gee[ert] = P.op("vector", "tensor_tensor", gsl,
                                      psEx[ert][0:m, 0:1024], gsl,
                                      ALU.bypass, sem=sDVE)
                    P.wait("vector", sDVE, v_gee[ert])
                for ert in range(2):
                    m = 128 if ert == 0 else 96
                    P.wait("tensor", sDVE, v_gee[ert])
                    vh = 0
                    for half in range(2):
                        for dt in range(32):
                            sm = sPE if (dt == 31 and half == 1) else None
                            v = P.op("tensor", "matmul",
                                     psEx[ert][0:m, half * 512:(half + 1) * 512],
                                     ar2(AT0 + dt * 224 + ert * 128,
                                         AT0 + dt * 224 + ert * 128 + m),
                                     ar1(dt * 1024 + half * 512,
                                         dt * 1024 + (half + 1) * 512),
                                     start=(dt == 0), stop=(dt == 31), sem=sm)
                            if sm:
                                vh = v
                    v_pe_lora = vh
                    P.wait("vector", sPE, vh)
                    v_hsc[ert] = P.op("vector", "tensor_tensor",
                                      AR2[0:m, HS0 + ert * 1024:
                                          HS0 + (ert + 1) * 1024],
                                      psEx[ert][0:m, 0:1024],
                                      AR2[0:m, GE0 + ert * 1024:
                                          GE0 + (ert + 1) * 1024],
                                      ALU.mult, sem=sDVE)

        if DEBUG and STOP_PHASE >= 7:
            P.wait("sync", sDVE, v_hsc[1])
            P.op("sync", "dma_start", s_dbg[:, 0:224], ar2(GN0, GN0 + 224),
                 sem=sS, n=16)
            P.op("sync", "dma_start", s_dbg[:, 1024:3072],
                 ar2(HS0, HS0 + 2048), sem=sS, n=16)
            P.op("sync", "dma_start", s_dbg[0:28, 3072:4096],
                 AR2[0:28, GT0:GT0 + 1024], sem=sS, n=16)

        v_pe_f = [0] * 32
        v_pe_st8 = [0] * 32
        v_mrb2 = 0
        if STOP_PHASE >= 8:
            st89 = ExitStack()
            psSx = st89.enter_context(nc.psum_tensor("ps8Sx", [128, 1024], F32))
            psSq = st89.enter_context(nc.psum_tensor("ps8Sq", [128, 1024], F32))
            st8i = ExitStack()
            psA = st8i.enter_context(nc.psum_tensor("ps8A", [128, 1024], F32))
            psB = st8i.enter_context(nc.psum_tensor("ps8B", [128, 1024], F32))
            if True:
                psF = [psA, psB]
                v_bw = [0] * 32
                v_hp8 = [0] * 32
                v_shp8 = [0] * 8
                for ot in range(32):
                    sl = ot % 2
                    w0 = WS0 + sl * 4096
                    if ot >= 2:
                        P.wait("gpsimd", sPE, v_pe_f[ot - 2])
                    v_bw[ot] = P.op(
                        "gpsimd", "dma_start",
                        ar2(w0, w0 + 4096).rearrange("p (dt o) -> p dt o", dt=32),
                        d_bwT.rearrange("(dt p) o -> p dt o", p=128)
                        [:, :, ot * 128:(ot + 1) * 128], sem=sG, n=16)
                    P.wait("tensor", sG, v_bw[ot])
                    if ot == 0:
                        P.wait("tensor", sDVE, v_hsc[1])
                    if ot >= 2:
                        P.wait("tensor", sDVE, v_hp8[ot - 2])
                    for half in range(2):
                        for dt in range(32):
                            P.op("tensor", "matmul",
                                 psF[sl][:, half * 512:(half + 1) * 512],
                                 ar2(w0 + dt * 128, w0 + (dt + 1) * 128),
                                 ar1(dt * 1024 + half * 512,
                                     dt * 1024 + (half + 1) * 512),
                                 start=(dt == 0), stop=False)
                        for ert in range(2):
                            kk = 128 if ert == 0 else 96
                            sm = sPE if (ert == 1 and half == 1) else None
                            v = P.op("tensor", "matmul",
                                     psF[sl][:, half * 512:(half + 1) * 512],
                                     AR2[0:kk, BL0 + ert * 4096 + ot * 128:
                                         BL0 + ert * 4096 + (ot + 1) * 128],
                                     AR2[0:kk, HS0 + ert * 1024 + half * 512:
                                         HS0 + ert * 1024 + (half + 1) * 512],
                                     start=False, stop=(ert == 1),
                                     skip_group_check=True, sem=sm)
                            if sm:
                                v_pe_f[ot] = v
                    P.wait("vector", sPE, v_pe_f[ot])
                    if ot == 0:
                        P.wait("vector", sS, v_shp[7] if STOP_PHASE >= 5 else 0)
                    if ot >= 1:
                        P.wait("vector", sPE, v_pe_st8[ot - 1])
                    if ot >= 4:
                        P.wait("vector", sS, v_shp8[(ot - 4) // 4])
                    hp = HPB[:, (ot % 4) * 1024:(ot % 4 + 1) * 1024]
                    dvs("tensor_tensor", hp, psF[sl][:, :],
                        ar1(ot * 1024, (ot + 1) * 1024), ALU.add)
                    v_hp8[ot] = dvs("tensor_tensor", SQB[:, :], hp, hp,
                                    ALU.mult)
                    P.wait("tensor", sDVE, v_hp8[ot])
                    for half in range(2):
                        P.op("tensor", "matmul",
                             psSx[0:1, half * 512:(half + 1) * 512],
                             ONE[:, 0:1], hp[:, half * 512:(half + 1) * 512],
                             start=(ot == 0), stop=(ot == 31),
                             skip_group_check=True)
                        sm = sPE if half == 1 else None
                        v = P.op("tensor", "matmul",
                                 psSq[0:1, half * 512:(half + 1) * 512],
                                 ONE[:, 0:1], SQB[:, half * 512:(half + 1) * 512],
                                 start=(ot == 0), stop=(ot == 31),
                                 skip_group_check=True, sem=sm)
                        if sm:
                            v_pe_st8[ot] = v
                    if ot % 4 == 3:
                        P.wait("sync", sDVE, v_hp8[ot])
                        v_shp8[ot // 4] = P.op(
                            "sync", "dma_start",
                            s_h2pre.rearrange("(o p) t -> p o t", p=128)
                            [:, ot - 3:ot + 1, :],
                            HPB[:, :].rearrange("p (o t) -> p o t", o=4),
                            sem=sS, n=16)
                st8i.close()
                v_mrb2 = _ln_finalize(P, psSx, psSq, v_pe_st8[31], SRW,
                                      BIA, ONF, sPE, sACT, sDVE, v_bia)

        if STOP_PHASE >= 9:
            with nc.psum_tensor("ps9T", [128, 2048], BF16) as psT8:
                v_h2 = [0] * 32
                v_hp9 = [0] * 32
                for i in range(34):
                    if i < 32:
                        if i < 4:
                            P.wait("sync", sPE, v_pe_st8[31])
                        else:
                            P.wait("sync", sDVE, v_h2[i - 4])
                        P.wait("sync", sS, v_shp8[i // 4])
                        v_hp9[i] = P.op(
                            "sync", "dma_start",
                            HPB[:, (i % 4) * 1024:(i % 4 + 1) * 1024],
                            s_h2pre[i * 128:(i + 1) * 128, :], sem=sS, n=16)
                    if i >= 2:
                        dt = i - 2
                        sl = dt % 4
                        P.wait("vector", sS, v_hp9[min(dt + 2, 31)])
                        if dt == 0:
                            P.wait("vector", sPE, v_pe_f[31])
                            P.wait("vector", sPE, v_mrb2)
                        hp = HPB[:, sl * 1024:(sl + 1) * 1024]
                        dvs("tensor_tensor", LTM[:, :], hp,
                            psSx[:, 0:1024], ALU.subtract)
                        v_h2[dt] = dvs("tensor_tensor",
                                       ar2(dt * 1024, (dt + 1) * 1024),
                                       LTM[:, :], psSq[:, 0:1024], ALU.mult)
                v_tr = {}
                v_trc = {}
                v_outd = [[0, 0] for _ in range(8)]
                for tt in range(8):
                    for hf in range(2):
                        for dd in range(16):
                            dt = hf * 16 + dd
                            ti = tt * 32 + dt
                            hi = tt * 2 + hf
                            if ti == 0:
                                P.wait("tensor", sDVE, v_h2[31])
                            if ti >= 2:
                                P.wait("tensor", sDVE, v_trc[ti - 2])
                            pT = psT8[:, (ti % 2) * 1024:(ti % 2) * 1024 + 128]
                            v_tr[ti] = P.op(
                                "tensor", "transpose", pT,
                                ar2(dt * 1024 + tt * 128,
                                    dt * 1024 + (tt + 1) * 128),
                                IDN[:, :], sem=sPE)
                            P.wait("vector", sPE, v_tr[ti])
                            if dd == 0 and hi >= 1:
                                ph, pf = divmod(hi - 1, 2)
                                P.wait("vector", sS, v_outd[ph][pf])
                            osl = ONB[:, dd * 128:(dd + 1) * 128]
                            v_trc[ti] = P.op(
                                "vector", "tensor_tensor", osl, pT, osl,
                                ALU.bypass, sem=sDVE)
                        P.wait("sync", sDVE, v_trc[tt * 32 + hf * 16 + 15])
                        v_outd[tt][hf] = P.op(
                            "sync", "dma_start",
                            d_out[tt * 128:(tt + 1) * 128,
                                  hf * 2048:(hf + 1) * 2048],
                            ONB[:, :], sem=sS, n=16)
                P.wait("sync", sS, v_outd[7][1])
            st89.close()

        P.wait("gpsimd", sG, sG.v)
        P.wait("sync", sS, sS.v)

        with nc.Block() as blk:
            @blk.tensor
            def _(E):
                _replay(E, P.streams["tensor"])

            @blk.vector
            def _(E):
                _replay(E, P.streams["vector"])

            @blk.scalar
            def _(E):
                _replay(E, P.streams["scalar"])

            @blk.gpsimd
            def _(E):
                _replay(E, P.streams["gpsimd"])

            @blk.sync
            def _(E):
                _replay(E, P.streams["sync"])

    return nc


def _ln_finalize(P, psSx, psSq, v_stats_done, SRW, BIA, ONF,
                 sPE, sACT, sDVE, v_bia):
    m = SRW[0:1, 0:1024]
    s2 = SRW[0:1, 1024:2048]
    s3 = SRW[0:1, 2048:3072]

    def dss(method, *args, **kwargs):
        v = P.op("vector", method, *args, sem=sDVE, **kwargs)
        P.wait("vector", sDVE, v)
        return v

    P.wait("vector", sPE, v_stats_done)
    dss("tensor_scalar", m, psSx[0:1, 0:1024], 1.0 / 4096.0, None, ALU.mult)
    v_rd = dss("tensor_scalar", s2, psSq[0:1, 0:1024], 1.0 / 4096.0, None,
               ALU.mult)
    dss("tensor_tensor", s3, m, m, ALU.mult)
    dss("tensor_tensor", s2, s2, s3, ALU.subtract)
    v1 = dss("tensor_scalar", s2, s2, 1e-5, None, ALU.add)
    P.wait("scalar", sDVE, max(v1, v_bia))
    va = P.op("scalar", "activation", s3, s2, AF.Ln, bias=BIA[0:1, 1:2],
              scale=1.0, sem=sACT)
    P.wait("scalar", sACT, va)
    v2 = P.op("scalar", "activation", s3, s3, AF.Exp, bias=BIA[0:1, 1:2],
              scale=-0.5, sem=sACT)
    P.wait("tensor", sDVE, v_rd)
    P.wait("tensor", sACT, v2)
    vb = 0
    for q, (src, ps) in enumerate([(m, psSx), (s3, psSq)]):
        for half in range(2):
            sm = sPE if (q == 1 and half == 1) else None
            v = P.op("tensor", "matmul", ps[:, half * 512:(half + 1) * 512],
                     ONF[0:1, :],
                     src[0:1, half * 512:(half + 1) * 512],
                     start=True, stop=True, sem=sm)
            if sm:
                vb = v
    return vb



def _prep_host(inputs):
    X = np.asarray(inputs["L0_attn_X"], dtype=np.float32)
    wq = np.ascontiguousarray(np.asarray(inputs["L0_attn_Wq"], np.float32))
    wk = np.ascontiguousarray(np.asarray(inputs["L0_attn_Wk"], np.float32))
    wv = np.ascontiguousarray(np.asarray(inputs["L0_attn_Wv"], np.float32))
    wo = np.ascontiguousarray(np.asarray(inputs["L0_attn_Wo"], np.float32))
    bwT = np.ascontiguousarray(np.asarray(inputs["L0_ffn_base_W"], np.float32).T)
    gwT = np.ascontiguousarray(np.asarray(inputs["L0_ffn_gate_W"], np.float32).T)
    aT = np.ascontiguousarray(
        np.asarray(inputs["L0_ffn_lora_A"], np.float32).reshape(ER, D).T)
    bl = np.ascontiguousarray(
        np.asarray(inputs["L0_ffn_lora_B"], np.float32).transpose(0, 2, 1)
        .reshape(ER, D))
    cosk, sink = _rope_tables()
    coskb = np.ascontiguousarray(cosk.astype(ml_dtypes.bfloat16))
    sinkb = np.ascontiguousarray(sink.astype(ml_dtypes.bfloat16))
    ident = np.eye(128, dtype=ml_dtypes.bfloat16)
    ones = np.ones((128, 128), dtype=ml_dtypes.bfloat16)
    E = np.zeros((NEXP, ER), dtype=np.float32)
    for ee in range(NEXP):
        E[ee, ee * RANK:(ee + 1) * RANK] = 2.0
    E = E.astype(ml_dtypes.bfloat16)

    xt_b = [np.ascontiguousarray(X[b].T) for b in range(2)]
    in_maps = []
    rows_all = []
    for c in range(8):
        b, j = c // 4, c % 4
        rows = own_rows(j)
        rows_all.append(rows)
        xtq = np.ascontiguousarray(X[b].T[:, rows])
        in_maps.append(dict(
            xt=xt_b[b], xtq=xtq, wq=wq, wk=wk, wv=wv, wo=wo,
            bwT=bwT, gwT=gwT, aT=aT, bl=bl,
            cosk=coskb, sink=sinkb,
            cosq=np.ascontiguousarray(coskb[:, rows]),
            sinq=np.ascontiguousarray(sinkb[:, rows]),
            masks=_masks_for(j), ident=ident, ones=ones, Emat=E,
        ))
    return in_maps, rows_all


def kernel(**inputs):
    global _last_results
    in_maps, rows_all = _prep_host(inputs)
    nc = build_program()
    res = run_bass_kernel_spmd(nc, in_maps, core_ids=list(range(8)))
    _last_results = res
    out = np.zeros((2, T, D), dtype=np.float32)
    for c in range(8):
        out[c // 4, rows_all[c]] = res.results[c]["out"]
    return out
